# version 06
# baseline (speedup 1.0000x reference)
import math
import numpy as np
import ml_dtypes
import concourse.bass as bass
import concourse.mybir as mybir
from concourse.bass_utils import run_bass_kernel_spmd
from contextlib import ExitStack

F32 = mybir.dt.float32
BF16 = mybir.dt.bfloat16
AF = mybir.ActivationFunctionType
ALU = mybir.AluOpType
AX = mybir.AxisListType
BF = ml_dtypes.bfloat16

NC = 8
D = 2048
DCH = 16
SEQ = 8192
TOK = 1024
RND = 512
E = 8192
H = 16
QKV = 2304
OCH = 18
RS = (28, 60)
NEG = -1.0e38
ISQD = 1.0 / math.sqrt(128.0)

ENGS = ("sync", "tensor", "vector", "scalar", "gpsimd")

_DBG = []
_STOP_AFTER = None
_last_results = None


class _Sem:
    __slots__ = ("h", "n", "name")

    def __init__(self, h, name):
        self.h, self.n, self.name = h, 0, name


class _Prog:
    def __init__(self, nc, es):
        self.nc, self.es = nc, es
        self.q = {e: [] for e in ENGS}
        self._lw = {e: {} for e in ENGS}
        self.nops = 0
        self.nbar = 0

    def sem(self, name):
        return _Sem(self.es.enter_context(self.nc.semaphore(name)), name)

    def op(self, eng, fn, waits=(), inc=None, k=1):
        lw = self._lw[eng]
        eff = []
        for t in waits:
            if t is None:
                continue
            s, v = t
            if v is None or lw.get(s.name, -1) >= v:
                continue
            eff.append((s, v))
            lw[s.name] = v
        tgt = None
        if inc is not None:
            inc.n += k
            tgt = inc.n

        def cl(e, fn=fn, eff=tuple(eff), inc=inc, k=k):
            for s, v in eff:
                e.wait_ge(s.h, v)
            r = fn(e)
            if inc is not None:
                r.then_inc(inc.h, k)

        self.q[eng].append(cl)
        self.nops += 1
        return (inc, tgt) if inc is not None else None

    def wait(self, eng, waits):
        self.op(eng, lambda e: e.nop(), waits=waits)

    def dma(self, q, out, in_, sem, waits=(), transpose=False):
        return self.op(
            q,
            lambda e, out=out, in_=in_, tr=transpose: e.dma_start(out=out, in_=in_, transpose=tr),
            waits=waits, inc=sem, k=16)

    def barrier(self):
        self.nbar += 1
        tgt = 5 * self.nbar
        for eng in ENGS:
            w = [(s, s.n) for s in self.dma_sems.get(eng, ())]
            self.op(eng, lambda e: e.nop(), waits=w, inc=self.BAR, k=1)
        for eng in ENGS:
            self.wait(eng, [(self.BAR, tgt)])

    def build(self, block):
        for eng in ENGS:
            ops = self.q[eng]
            if not ops:
                continue

            def run(e, ops=ops):
                for cl in ops:
                    cl(e)

            getattr(block, eng)(run)


def build_nc():
    nc = bass.Bass("TRN2", target_bir_lowering=False)

    d_x = nc.dram_tensor("x", [TOK, D], F32, kind="ExternalInput")
    d_wqkv = nc.dram_tensor("wqkv", [D, QKV], F32, kind="ExternalInput")
    d_bqkv = nc.dram_tensor("bqkv2d", [128, OCH], F32, kind="ExternalInput")
    d_wfc = nc.dram_tensor("wfc", [D, D], F32, kind="ExternalInput")
    d_bfc = nc.dram_tensor("bfc2d", [128, DCH], F32, kind="ExternalInput")
    d_wu = nc.dram_tensor("wu", [E, D], F32, kind="ExternalInput")
    d_wg = nc.dram_tensor("wg", [E, D], F32, kind="ExternalInput")
    d_wd = nc.dram_tensor("wd", [D, E], F32, kind="ExternalInput")
    d_wu_sh = nc.dram_tensor("wu_sh", [E // NC, D], F32, kind="ExternalInput")
    d_wg_sh = nc.dram_tensor("wg_sh", [E // NC, D], F32, kind="ExternalInput")
    d_wd_sh = nc.dram_tensor("wd_sh", [D // NC, E], F32, kind="ExternalInput")
    d_ident = nc.dram_tensor("ident", [128, 128], F32, kind="ExternalInput")
    d_identb = nc.dram_tensor("identb", [128, 128], BF16, kind="ExternalInput")
    d_dmask = nc.dram_tensor("dmask", [4, 128, RND], BF16, kind="ExternalInput")
    d_padc = nc.dram_tensor("padcol", [128, 192], F32, kind="ExternalInput")
    d_out = nc.dram_tensor("xout", [TOK, D], F32, kind="ExternalOutput")
    d_dbg = {t: nc.dram_tensor("dbg_" + t, [128, DCH * TOK], F32, kind="ExternalOutput")
             for t in _DBG}
    if _DBG:
        d_dbg_tug = nc.dram_tensor("dbg_tug", [2048, D], BF16, kind="ExternalOutput")
        d_dbg_td = nc.dram_tensor("dbg_td", [D, 1024], BF16, kind="ExternalOutput")
        d_dbg_ssc = nc.dram_tensor("dbg_ssc", [128, 10], F32, kind="ExternalOutput")
        d_dbg_ars = nc.dram_tensor("dbg_ars", [1, 24], F32, kind="ExternalOutput")
        d_dbg_cells = nc.dram_tensor("dbg_cells", [1, 16], F32, kind="ExternalOutput")
        d_dbg_throw = nc.dram_tensor("dbg_throw", [1, 10], F32, kind="ExternalOutput")
        d_dbg_kc = nc.dram_tensor("dbg_kc", [1, 16], F32, kind="ExternalOutput")
        d_dbg_neg = nc.dram_tensor("dbg_neg", [1, 1024], BF16, kind="ExternalOutput")
        d_dbg_lrow = nc.dram_tensor("dbg_lrow", [1, 1024], F32, kind="ExternalOutput")
        d_dbg_qn = nc.dram_tensor("dbg_qn", [H, TOK], F32, kind="ExternalOutput")
        d_dbg_kv = nc.dram_tensor("dbg_kv", [128, TOK], BF16, kind="ExternalOutput")
        d_dbg_ownk = nc.dram_tensor("dbg_ownk", [128, TOK], BF16, kind="ExternalOutput")
        d_dbg_xn2 = nc.dram_tensor("dbg_xn2", [128, DCH * TOK], BF16, kind="ExternalOutput")
        d_dbg_ohi = nc.dram_tensor("dbg_ohi", [2, 128, H * 512], BF16, kind="ExternalOutput")
        d_dbg_olo = nc.dram_tensor("dbg_olo", [2, 128, H * 512], BF16, kind="ExternalOutput")
        d_dbg_ownv = nc.dram_tensor("dbg_ownv", [128, TOK], BF16, kind="ExternalOutput")
        d_dbg_wqh = nc.dram_tensor("dbg_wqh", [128, 2 * 384], BF16, kind="ExternalOutput")

    d_tug = nc.dram_tensor("tug_nat", [2 * E, D], BF16)
    d_td = nc.dram_tensor("td_nat", [D, E], BF16)
    d_arin = nc.dram_tensor("ar_in", [3, 8], F32)
    d_arout = nc.dram_tensor("ar_out", [3, 8], F32)
    d_kvsh = nc.dram_tensor("kv_sh", [4, 128, TOK], BF16)
    d_kvall = nc.dram_tensor("kv_all", [4 * NC, 128, TOK], BF16)
    d_qn = nc.dram_tensor("qn_dram", [H, TOK], F32)
    d_qsp = nc.dram_tensor("qspill", [2, 2, H, 128, RND], BF16)

    es = ExitStack()
    p = _Prog(nc, es)
    SY = p.sem("sy"); SC = p.sem("sc"); SG = p.sem("sg")
    QW = [p.sem(f"qw{i}") for i in range(4)]
    MM = p.sem("mm"); VE = p.sem("ve"); ACt = p.sem("ac"); CC = p.sem("cc")
    p.BAR = p.sem("bar")
    p.dma_sems = {"sync": (SY,), "scalar": (SC,), "gpsimd": (SG,)}

    sb = lambda n, s, d: es.enter_context(nc.sbuf_tensor(n, s, d))
    X = sb("X", [128, DCH * TOK], F32)
    ones_cf = sb("ones_cf", [128, 1], F32)
    ones_rf = sb("ones_rf", [1, 128], F32)
    ones_cb = sb("ones_cb", [128, 1], BF16)
    ones_rb = sb("ones_rb", [1, 128], BF16)
    ident = sb("s_ident", [128, 128], F32)
    identb = sb("s_identb", [128, 128], BF16)
    ssc = sb("ssc", [128, 10], F32)
    bqs = sb("bqs", [128, OCH], F32)
    bfc = sb("bfc", [128, DCH], F32)
    ownKh = sb("ownKh", [128, TOK], BF16)
    ownKl = sb("ownKl", [128, TOK], BF16)
    ownVh = sb("ownVh", [128, TOK], BF16)
    ownVl = sb("ownVl", [128, TOK], BF16)
    cells = sb("cells", [1, 16], F32)
    throw = sb("throw", [1, 10], F32)

    PS = [es.enter_context(nc.psum_tensor(f"ps{i}", [128, 512], F32)) for i in range(8)]
    pstok = [None] * 8

    p.op("vector", lambda e: e.memset(ones_cf[:, :], 1.0))
    p.op("vector", lambda e: e.memset(ones_rf[:, :], 1.0))
    p.op("vector", lambda e: e.memset(ones_cb[:, :], 1.0))
    p.op("vector", lambda e: e.memset(cells[:, :], 0.0))
    t_ones = p.op("vector", lambda e: e.memset(ones_rb[:, :], 1.0), inc=VE)
    t_id = p.dma("sync", ident[:, :], d_ident[:, :], SY)
    t_idb = p.dma("sync", identb[:, :], d_identb[:, :], SY)
    t_bq0 = p.dma("sync", bqs[:, :], d_bqkv[:, :], SY)
    t_bfc = p.dma("sync", bfc[:, :], d_bfc[:, :], SY)
    t_bq = p.op("scalar", lambda e: e.activation(bqs[:, 0:H], bqs[:, 0:H], AF.Identity, scale=ISQD),
                waits=[t_bq0], inc=ACt)

    with (nc.sbuf_tensor("xrow", [128, 2 * D], F32) as xrow,
          nc.sbuf_tensor("ashard", [128, E], F32) as ash,
          nc.sbuf_tensor("racc", [128, 20], F32) as racc,
          nc.sbuf_tensor("acc3", [128, 3], F32) as acc3,
          nc.sbuf_tensor("ones8", [128, 8], F32) as ones8,
          nc.sbuf_tensor("pa", [3, 8], F32) as pa,
          nc.sbuf_tensor("arsrow", [1, 24], F32) as arsrow):
        rowtok = [None, None]
        for rc in range(8):
            b = rc % 2
            t_l = p.dma("sync", xrow[:, b * D:(b + 1) * D], d_x[rc * 128:(rc + 1) * 128, :], SY,
                        waits=[rowtok[b]])
            for dc in range(DCH):
                psi = 6 + (dc % 2)
                t_tp = p.op("tensor",
                            lambda e, b=b, dc=dc, psi=psi: e.transpose(
                                PS[psi][:, 0:128],
                                xrow[:, b * D + dc * 128: b * D + (dc + 1) * 128], ident[:, :]),
                            waits=[t_l, t_id, pstok[psi]], inc=MM)
                t_ev = p.op("scalar",
                            lambda e, rc=rc, dc=dc, psi=psi: e.copy(
                                X[:, dc * TOK + rc * 128: dc * TOK + (rc + 1) * 128],
                                PS[psi][:, 0:128]),
                            waits=[t_tp], inc=ACt)
                pstok[psi] = t_ev
                if dc == DCH - 1:
                    rowtok[b] = t_ev
        t_X0 = (ACt, ACt.n)

        p.op("vector", lambda e: e.memset(ones8[:, :], 1.0))
        shtok = [None, None]
        col = 0
        t_r = None
        tiles_a = []
        for k in range(8):
            tiles_a.append(d_wu_sh[k * 128:(k + 1) * 128, :])
        for k in range(8):
            tiles_a.append(d_wg_sh[k * 128:(k + 1) * 128, :])
        for k in range(2):
            for cs in range(2):
                tiles_a.append(d_wd_sh[k * 128:(k + 1) * 128, cs * 4096:(cs + 1) * 4096])
        for src in tiles_a:
            b = col % 2
            fd = src.shape[1]
            t_l = p.dma("scalar", ash[:, b * 4096:b * 4096 + fd], src, SC, waits=[shtok[b]])
            t_r = p.op("vector",
                       lambda e, b=b, col=col, fd=fd: e.reduce_sum(
                           racc[:, col:col + 1], ash[:, b * 4096:b * 4096 + fd],
                           AX.X, apply_absolute_value=True),
                       waits=[t_l], inc=VE)
            shtok[b] = t_r
            col += 1
        p.op("vector", lambda e: e.reduce_sum(acc3[:, 0:1], racc[:, 0:8], AX.X), waits=[t_r])
        p.op("vector", lambda e: e.reduce_sum(acc3[:, 1:2], racc[:, 8:16], AX.X))
        t_ac = p.op("vector", lambda e: e.reduce_sum(acc3[:, 2:3], racc[:, 16:20], AX.X), inc=VE)
        t_pa = p.op("tensor", lambda e: e.matmul(PS[0][0:3, 0:8], acc3[:, :], ones8[:, :],
                                                 start=True, stop=True),
                    waits=[t_ac, pstok[0]], inc=MM)
        t_pac = p.op("scalar", lambda e: e.copy(pa[:, :], PS[0][0:3, 0:8]), waits=[t_pa], inc=ACt)
        pstok[0] = t_pac
        t_pad = p.dma("gpsimd", d_arin[:, :], pa[:, :], SG, waits=[t_pac])
        t_ar = p.op("gpsimd",
                    lambda e: e.collective_compute(
                        "AllReduce", ALU.add, replica_groups=[list(range(NC))],
                        ins=[d_arin[:, :]], outs=[d_arout[:, :]]),
                    waits=[t_pad], inc=CC)
        t_arl = p.dma("gpsimd", arsrow[:, :], d_arout[:, :], SG, waits=[t_ar])
        C = 1.0 / (1 << 24)
        tk = t_arl
        steps = [
            lambda e: e.tensor_single_scalar(cells[:, 4:5], arsrow[:, 0:1], C, ALU.mult),
            lambda e: e.tensor_single_scalar(cells[:, 4:5], cells[:, 4:5], 1e-8, ALU.add),
            lambda e: e.tensor_single_scalar(cells[:, 5:6], arsrow[:, 8:9], C, ALU.mult),
            lambda e: e.tensor_single_scalar(cells[:, 5:6], cells[:, 5:6], 1e-8, ALU.add),
            lambda e: e.tensor_single_scalar(cells[:, 6:7], arsrow[:, 16:17], C, ALU.mult),
            lambda e: e.tensor_single_scalar(cells[:, 6:7], cells[:, 6:7], 1e-8, ALU.add),
            lambda e: e.tensor_single_scalar(throw[:, 0:1], cells[:, 4:5], 0.5, ALU.mult),
            lambda e: e.tensor_single_scalar(throw[:, 1:2], cells[:, 5:6], 0.5, ALU.mult),
            lambda e: e.tensor_single_scalar(throw[:, 7:8], cells[:, 6:7], 0.5, ALU.mult),
            lambda e: e.tensor_single_scalar(throw[:, 2:3], throw[:, 7:8], 0.5, ALU.mult),
            lambda e: e.tensor_copy(throw[:, 3:4], throw[:, 0:1]),
            lambda e: e.tensor_single_scalar(throw[:, 4:5], throw[:, 0:1], -1.0, ALU.mult),
            lambda e: e.tensor_copy(throw[:, 5:6], throw[:, 1:2]),
            lambda e: e.tensor_single_scalar(throw[:, 6:7], throw[:, 1:2], -1.0, ALU.mult),
            lambda e: e.tensor_single_scalar(throw[:, 8:9], throw[:, 7:8], -1.0, ALU.mult),
        ]
        for fn in steps:
            tk = p.op("vector", fn, waits=[tk], inc=VE)
        t_th = tk
        t_bc = p.op("tensor", lambda e: e.matmul(PS[1][:, 0:10], ones_rf[:, :], throw[:, :],
                                                 start=True, stop=True),
                    waits=[t_th, t_ones, pstok[1]], inc=MM)
        t_ssc = p.op("scalar", lambda e: e.copy(ssc[:, :], PS[1][:, 0:10]), waits=[t_bc], inc=ACt)
        pstok[1] = t_ssc
        if _DBG:
            p.dma("gpsimd", d_dbg_ssc[:, :], ssc[:, :], SG, waits=[t_ssc])
            p.dma("gpsimd", d_dbg_ars[:, :], arsrow[:, :], SG, waits=[t_arl])
            p.dma("gpsimd", d_dbg_cells[:, :], cells[:, :], SG, waits=[t_th])
            p.dma("gpsimd", d_dbg_throw[:, :], throw[:, :], SG, waits=[t_th])
    p.barrier()

    qstate = dict(ldtok=[None, None], ttok=[None, None], t_sg=None, qi=0)

    def quant_tiles(tiles, qbufs, qsem):
        wld, s1b, s2b, tb = qbufs
        tok = None
        for src_ap, dst_ap, thc, nthc in tiles:
            b = qstate["qi"] % 2
            t_l = p.dma("gpsimd", wld[:, b * 1024:(b + 1) * 1024], src_ap, SG,
                        waits=[qstate["ldtok"][b]])
            t_s1 = p.op("scalar", lambda e, b=b, nthc=nthc: e.sign(
                s1b[:, :], wld[:, b * 1024:(b + 1) * 1024], bias=ssc[:, nthc:nthc + 1]),
                waits=[t_l, t_ssc, qstate["t_sg"]], inc=ACt)
            t_s2 = p.op("scalar", lambda e, b=b, thc=thc: e.sign(
                s2b[:, :], wld[:, b * 1024:(b + 1) * 1024], bias=ssc[:, thc:thc + 1]),
                waits=[], inc=ACt)
            qstate["ldtok"][b] = t_s2
            t_t = p.op("vector", lambda e, b=b: e.tensor_tensor(
                tb[:, b * 1024:(b + 1) * 1024], s1b[:, :], s2b[:, :], ALU.add),
                waits=[t_s2, qstate["ttok"][b]], inc=VE)
            qstate["t_sg"] = t_t
            tok = p.dma("gpsimd", dst_ap, tb[:, b * 1024:(b + 1) * 1024], qsem, waits=[t_t])
            qstate["ttok"][b] = tok
            qstate["qi"] += 1
        return [tok]

    def tug_group_tiles(g):
        tiles = []
        for rc in range(4 * g, 4 * g + 4):
            for ch in range(2):
                dst_r = 512 * (rc // 2) + 128 * (rc % 2)
                tiles.append((d_wu[rc * 128:(rc + 1) * 128, ch * 1024:(ch + 1) * 1024],
                              d_tug[dst_r:dst_r + 128, ch * 1024:(ch + 1) * 1024], 3, 4))
        for rc in range(4 * g, 4 * g + 4):
            for ch in range(2):
                dst_r = 512 * (rc // 2) + 256 + 128 * (rc % 2)
                tiles.append((d_wg[rc * 128:(rc + 1) * 128, ch * 1024:(ch + 1) * 1024],
                              d_tug[dst_r:dst_r + 128, ch * 1024:(ch + 1) * 1024], 5, 6))
        return tiles

    def td_cs_tiles(g):
        if g % 2 == 1:
            return []
        cs = g // 2
        return [(d_wd[rc * 128:(rc + 1) * 128, cs * 1024:(cs + 1) * 1024],
                 d_td[rc * 128:(rc + 1) * 128, cs * 1024:(cs + 1) * 1024], 7, 8)
                for rc in range(D // 128)]

    def ln_emit(xnh, xnl, r, coloff, bufs, t_extra=()):
        sq, mrow, vrow, srow, rrow, trow, wrow, Brs, Bmr, xt1, xt2 = (
            bufs["sq"], bufs["mrow"], bufs["vrow"], bufs["srow"], bufs["rrow"],
            bufs["trow"], bufs["wrow"], bufs["Brs"], bufs["Bmr"], bufs["xt1"], bufs["xt2"])
        sqtok = [None]
        t_s2m = None
        for dc in range(DCH):
            xs = X[:, dc * TOK + r * RND: dc * TOK + r * RND + RND]
            t_sq = p.op("scalar", lambda e, xs=xs: e.square(sq[:, :], xs),
                        waits=[sqtok[0], t_X0] + list(t_extra), inc=ACt)
            p.op("tensor", lambda e, dc=dc, xs=xs: e.matmul(
                PS[6][0:1, :], ones_cf[:, :], xs, start=(dc == 0), stop=(dc == DCH - 1)),
                waits=[pstok[6], t_X0] if dc == 0 else [])
            t_s2m = p.op("tensor", lambda e, dc=dc: e.matmul(
                PS[7][0:1, :], ones_cf[:, :], sq[:, :],
                start=(dc == 0), stop=(dc == DCH - 1)),
                waits=([pstok[7]] if dc == 0 else []) + [t_sq], inc=MM)
            sqtok[0] = t_s2m
        IC = 1.0 / 2048.0
        tk = p.op("vector", lambda e: e.tensor_single_scalar(mrow[:, :], PS[6][0:1, :], IC, ALU.mult),
                  waits=[t_s2m], inc=VE)
        tk = p.op("vector", lambda e: e.tensor_single_scalar(vrow[:, :], PS[7][0:1, :], IC, ALU.mult),
                  waits=[tk], inc=VE)
        tk = p.op("vector", lambda e: e.tensor_tensor(trow[:, :], mrow[:, :], mrow[:, :], ALU.mult),
                  waits=[tk], inc=VE)
        tk = p.op("vector", lambda e: e.tensor_tensor(wrow[:, :], vrow[:, :], trow[:, :], ALU.subtract),
                  waits=[tk], inc=VE)
        t_ve = p.op("vector", lambda e: e.tensor_single_scalar(vrow[:, :], wrow[:, :], 1e-5, ALU.add),
                    waits=[tk], inc=VE)
        t_prev = p.op("scalar", lambda e: e.activation(srow[:, :], vrow[:, :], AF.Sqrt), waits=[t_ve], inc=ACt)
        for _ in range(2):
            t_prev = p.op("vector", lambda e: e.reciprocal(rrow[:, :], srow[:, :]), waits=[t_prev], inc=VE)
            t_prev = p.op("vector", lambda e: e.tensor_tensor(trow[:, :], vrow[:, :], rrow[:, :], ALU.mult),
                          waits=[t_prev], inc=VE)
            t_prev = p.op("vector", lambda e: e.tensor_tensor(wrow[:, :], trow[:, :], srow[:, :], ALU.add),
                          waits=[t_prev], inc=VE)
            t_prev = p.op("vector", lambda e: e.tensor_single_scalar(srow[:, :], wrow[:, :], 0.5, ALU.mult),
                          waits=[t_prev], inc=VE)
        t_prev = p.op("vector", lambda e: e.reciprocal(rrow[:, :], srow[:, :]), waits=[t_prev], inc=VE)
        t_mr = p.op("vector", lambda e: e.tensor_tensor(trow[:, :], mrow[:, :], rrow[:, :], ALU.mult),
                    waits=[t_prev], inc=VE)
        t_b1 = p.op("tensor", lambda e: e.matmul(PS[6][:, :], ones_rf[:, :], rrow[:, :],
                                                 start=True, stop=True), waits=[t_mr, pstok[6]], inc=MM)
        t_b2 = p.op("tensor", lambda e: e.matmul(PS[7][:, :], ones_rf[:, :], trow[:, :],
                                                 start=True, stop=True), waits=[t_mr, pstok[7]], inc=MM)
        t_c1 = p.op("scalar", lambda e: e.copy(Brs[:, :], PS[6][:, :]), waits=[t_b1], inc=ACt)
        t_c2 = p.op("scalar", lambda e: e.copy(Bmr[:, :], PS[7][:, :]), waits=[t_b2], inc=ACt)
        pstok[6] = t_c1
        pstok[7] = t_c2
        t_xn = None
        for dc in range(DCH):
            xs = X[:, dc * TOK + r * RND: dc * TOK + r * RND + RND]
            c0 = coloff(dc)
            p.op("vector", lambda e, xs=xs: e.tensor_tensor(xt1[:, :], xs, Brs[:, :], ALU.mult),
                 waits=[t_c1, t_c2])
            p.op("vector", lambda e: e.tensor_tensor(xt2[:, :], xt1[:, :], Bmr[:, :], ALU.subtract))
            p.op("vector", lambda e, c0=c0: e.tensor_copy(xnh[:, c0:c0 + RND], xt2[:, :]),
                 waits=list(t_extra))
            t_xn = p.op("vector", lambda e, c0=c0: e.tensor_tensor(
                xnl[:, c0:c0 + RND], xt2[:, :], xnh[:, c0:c0 + RND], ALU.subtract), inc=VE)
        return t_xn

    def ffn_emit(r, xnh, xnl, t_xn, bufs, quant_cb=None, qug=None, qtd=None):
        ugb, tdb, silub, hfb, hgh, hgl = (bufs["ugb"], bufs["tdb"], bufs["silub"],
                                          bufs["hfb"], bufs["hgh"], bufs["hgl"])
        ugtok = bufs["ugtok"]; tdtok = bufs["tdtok"]; htok = bufs["htok"]
        t_ugfinal = None
        for g in range(16):
            if quant_cb is not None:
                quant_cb(g)
            tdld = []
            for ecl in range(4):
                ec = 4 * g + ecl
                tdld.append(p.dma("scalar", tdb[:, ecl * 2048:(ecl + 1) * 2048],
                                  d_td[0:D, ec * 128:(ec + 1) * 128], SC,
                                  waits=[tdtok[0]] + (qtd[g] if qtd else []), transpose=True))
            t_hready = None
            for s2 in range(2):
                ss = 2 * g + s2
                b = ss % 2
                t_ug = None
                for dc in range(DCH):
                    t_ug = p.dma("sync", ugb[b][:, dc * 512:(dc + 1) * 512],
                                 d_tug[512 * ss:512 * ss + 512, dc * 128:(dc + 1) * 128], SY,
                                 waits=[ugtok[b]] + (qug[g] if qug else []), transpose=True)
                mmt = [None] * 4
                for dc in range(DCH):
                    for pi, off in ((0, 0), (1, 128), (2, 256), (3, 384)):
                        w0 = [pstok[pi], t_ug, t_xn] if dc == 0 else []
                        p.op("tensor", lambda e, pi=pi, off=off, dc=dc, b=b: e.matmul(
                            PS[pi][:, :], ugb[b][:, dc * 512 + off: dc * 512 + off + 128],
                            xnh[:, dc * 512:(dc + 1) * 512],
                            start=(dc == 0), stop=False), waits=w0)
                        mmt[pi] = p.op("tensor", lambda e, pi=pi, off=off, dc=dc, b=b: e.matmul(
                            PS[pi][:, :], ugb[b][:, dc * 512 + off: dc * 512 + off + 128],
                            xnl[:, dc * 512:(dc + 1) * 512],
                            start=False, stop=(dc == DCH - 1)), inc=MM)
                ugtok[b] = mmt[3]
                t_ugfinal = mmt[3]
                for ecl in range(2):
                    hsl = 2 * s2 + ecl
                    pu, pg = ecl, 2 + ecl
                    bb = hsl % 2
                    t_si = p.op("scalar", lambda e, pu=pu, bb=bb: e.activation(
                        silub[:, bb * 512:(bb + 1) * 512], PS[pu][:, :], AF.Silu,
                        scale=ssc[:, 0:1]), waits=[mmt[pu], bufs["silutok"][bb], t_ssc], inc=ACt)
                    pstok[pu] = t_si
                    t_hf = p.op("vector", lambda e, pg=pg, bb=bb: e.scalar_tensor_tensor(
                        hfb[:, bb * 512:(bb + 1) * 512], PS[pg][:, :], ssc[:, 1:2],
                        silub[:, bb * 512:(bb + 1) * 512], ALU.mult, ALU.mult),
                        waits=[mmt[pg], t_si, bufs["hftok"][bb]], inc=VE)
                    pstok[pg] = t_hf
                    bufs["silutok"][bb] = t_hf
                    p.op("vector", lambda e, hsl=hsl, bb=bb: e.tensor_copy(
                        hgh[:, hsl * 512:(hsl + 1) * 512], hfb[:, bb * 512:(bb + 1) * 512]),
                        waits=[htok[0]])
                    t_hl = p.op("vector", lambda e, hsl=hsl, bb=bb: e.tensor_tensor(
                        hgl[:, hsl * 512:(hsl + 1) * 512], hfb[:, bb * 512:(bb + 1) * 512],
                        hgh[:, hsl * 512:(hsl + 1) * 512], ALU.subtract), inc=VE)
                    bufs["hftok"][bb] = t_hl
                    if hsl == 3:
                        t_hready = t_hl
            t_dlast = None
            for dm in range(DCH):
                pi = 4 + dm % 2
                t_d = None
                for ecl in range(4):
                    w0 = [pstok[pi], tdld[ecl], t_hready] if ecl == 0 else []
                    p.op("tensor", lambda e, pi=pi, ecl=ecl, dm=dm: e.matmul(
                        PS[pi][:, :], tdb[:, ecl * 2048 + dm * 128: ecl * 2048 + (dm + 1) * 128],
                        hgh[:, ecl * 512:(ecl + 1) * 512], start=(ecl == 0), stop=False),
                        waits=w0)
                    t_d = p.op("tensor", lambda e, pi=pi, ecl=ecl, dm=dm: e.matmul(
                        PS[pi][:, :], tdb[:, ecl * 2048 + dm * 128: ecl * 2048 + (dm + 1) * 128],
                        hgl[:, ecl * 512:(ecl + 1) * 512], start=False, stop=(ecl == 3)),
                        inc=MM)
                t_fl = p.op("vector", lambda e, pi=pi, dm=dm, r=r: e.scalar_tensor_tensor(
                    X[:, dm * TOK + r * RND: dm * TOK + r * RND + RND], PS[pi][:, :],
                    ssc[:, 2:3], X[:, dm * TOK + r * RND: dm * TOK + r * RND + RND],
                    ALU.mult, ALU.add), waits=[t_d], inc=VE)
                pstok[pi] = t_fl
                t_dlast = t_d
            tdtok[0] = t_dlast
            htok[0] = t_dlast
        return t_ugfinal

    def ffn_phase(tag):
        ps = ExitStack()
        T = lambda n, s, d: ps.enter_context(nc.sbuf_tensor(n + tag, s, d))
        do_quant = (tag == "ffn1")
        if do_quant:
            qbufs = (T("wld", [128, 2 * 1024], F32), T("s1b", [128, 1024], BF16),
                     T("s2b", [128, 1024], BF16), T("tb", [128, 2 * 1024], BF16))
            qug = [None] * 16
            qtd = [None] * 16
            qug[0] = quant_tiles(tug_group_tiles(0), qbufs, QW[0])
            qtd[0] = quant_tiles(td_cs_tiles(0), qbufs, QW[2])
            qtd[1] = qtd[0]
            qug[1] = quant_tiles(tug_group_tiles(1), qbufs, QW[1])

            def quant_cb(g):
                if g + 2 <= 15:
                    qug[g + 2] = quant_tiles(tug_group_tiles(g + 2), qbufs, QW[(g + 2) % 2])
                    tiles = td_cs_tiles(g + 2)
                    if tiles:
                        tt = quant_tiles(tiles, qbufs, QW[2 + ((g + 2) // 2) % 2])
                        qtd[g + 2] = tt
                        if g + 3 <= 15:
                            qtd[g + 3] = tt
        else:
            qug = qtd = quant_cb = None
        xnh = T("xnh", [128, DCH * RND], BF16)
        xnl = T("xnl", [128, DCH * RND], BF16)
        ugb0 = T("ugb0", [128, DCH * 512], BF16)
        ugb1 = T("ugb1", [128, DCH * 512], BF16)
        tdb = T("tdb", [128, 4 * 2048], BF16)
        hgh = T("hgh", [128, 4 * 512], BF16)
        hgl = T("hgl", [128, 4 * 512], BF16)
        silub = T("silub", [128, 1024], F32)
        hfb = T("hfb", [128, 1024], F32)
        xt1 = T("xt1", [128, 512], F32)
        sq = xt1
        xt2 = T("xt2", [128, 512], F32)
        Brs = T("Brs", [128, 512], F32)
        Bmr = T("Bmr", [128, 512], F32)
        mrow = T("mrow", [1, 512], F32)
        vrow = T("vrow", [1, 512], F32)
        srow = T("srow", [1, 512], F32)
        rrow = T("rrow", [1, 512], F32)
        trow = T("trow", [1, 512], F32)
        wrow = T("wrow", [1, 512], F32)
        lnbufs = dict(sq=sq, mrow=mrow, vrow=vrow, srow=srow, rrow=rrow, trow=trow,
                      wrow=wrow, Brs=Brs, Bmr=Bmr, xt1=xt1, xt2=xt2)
        fbufs = dict(ugb=[ugb0, ugb1], tdb=tdb, silub=silub, hfb=hfb, hgh=hgh, hgl=hgl,
                     ugtok=[None, None], tdtok=[None], htok=[None],
                     silutok=[None, None], hftok=[None, None])
        t_prev = ()
        for r in range(2):
            t_xn = ln_emit(xnh, xnl, r, lambda dc: dc * RND, lnbufs, t_extra=t_prev)
            t_ug = ffn_emit(r, xnh, xnl, t_xn, fbufs,
                            quant_cb=quant_cb if (do_quant and r == 0) else None,
                            qug=qug, qtd=qtd)
            t_prev = (t_ug,)
        ps.close()
        p.barrier()

    ffn_phase("ffn1")
    if "ffn1" in d_dbg:
        p.dma("gpsimd", d_dbg["ffn1"][:, :], X[:, :], SG)
        p.dma("gpsimd", d_dbg_tug[:, :], d_tug[0:2048, :], SG)
        p.dma("gpsimd", d_dbg_td[:, :], d_td[:, 0:1024], SG)

    ps_qkv = ExitStack()
    if _STOP_AFTER != "ffn1":
        Tq = lambda n, s, d: ps_qkv.enter_context(nc.sbuf_tensor(n, s, d))
        xnh2 = Tq("xnh2", [128, DCH * TOK], BF16)
        xnl2 = Tq("xnl2", [128, DCH * TOK], BF16)
        wqf = Tq("wqf", [128, 2 * 384], F32)
        wqh = Tq("wqh", [128, 2 * 384], BF16)
        wql = Tq("wql", [128, 2 * 384], BF16)
        qf = Tq("qf", [128, 2 * 1024], F32)
        sqq = Tq("sqq", [128, 1024], F32)
        qhib = Tq("qhib", [128, 1024], BF16)
        qlob = Tq("qlob", [128, 1024], BF16)
        qrow = Tq("qrow", [1, 512], F32)
        sq2 = Tq("sq2", [128, 512], F32)
        xt1b = Tq("xt1b", [128, 512], F32)
        xt2b = Tq("xt2b", [128, 512], F32)
        Brs2 = Tq("Brs2", [128, 512], F32)
        Bmr2 = Tq("Bmr2", [128, 512], F32)
        m2 = Tq("m2", [1, 512], F32)
        v2 = Tq("v2", [1, 512], F32)
        s2r = Tq("s2r", [1, 512], F32)
        r2 = Tq("r2", [1, 512], F32)
        t2r = Tq("t2r", [1, 512], F32)
        w2r = Tq("w2r", [1, 512], F32)
        lnbufs2 = dict(sq=sq2, mrow=m2, vrow=v2, srow=s2r, rrow=r2, trow=t2r, wrow=w2r,
                       Brs=Brs2, Bmr=Bmr2, xt1=xt1b, xt2=xt2b)
        t_xn2a = ln_emit(xnh2, xnl2, 0, lambda dc: dc * TOK, lnbufs2)
        t_xn2b = ln_emit(xnh2, xnl2, 1, lambda dc: dc * TOK + RND, lnbufs2)

        wtok = [None, None]
        sgl = None
        qftok = None
        for og in range(6):
            mmend = [None] * 6
            for dc in range(DCH):
                b = dc % 2
                t_l = p.dma("sync", wqf[:, b * 384:(b + 1) * 384],
                            d_wqkv[dc * 128:(dc + 1) * 128, og * 384:(og + 1) * 384], SY,
                            waits=[wtok[b]])
                t_cp = p.op("vector", lambda e, b=b: e.tensor_copy(wqh[:, b * 384:(b + 1) * 384],
                                                                    wqf[:, b * 384:(b + 1) * 384]),
                            waits=[t_l], inc=VE)
                t_sp = p.op("vector", lambda e, b=b: e.tensor_tensor(
                    wql[:, b * 384:(b + 1) * 384], wqf[:, b * 384:(b + 1) * 384],
                    wqh[:, b * 384:(b + 1) * 384], ALU.subtract), waits=[t_cp], inc=VE)
                for ocl in range(3):
                    for th in range(2):
                        pi = ocl * 2 + th
                        woff = b * 384 + ocl * 128
                        rhsh = xnh2[:, dc * TOK + th * RND: dc * TOK + th * RND + RND]
                        rhsl = xnl2[:, dc * TOK + th * RND: dc * TOK + th * RND + RND]
                        w0 = [pstok[pi], t_sp, t_xn2a, t_xn2b] if dc == 0 else [t_sp]
                        p.op("tensor", lambda e, pi=pi, woff=woff, rhsh=rhsh, dc=dc: e.matmul(
                            PS[pi][:, :], wqh[:, woff:woff + 128], rhsh,
                            start=(dc == 0), stop=False), waits=w0)
                        p.op("tensor", lambda e, pi=pi, woff=woff, rhsl=rhsl: e.matmul(
                            PS[pi][:, :], wqh[:, woff:woff + 128], rhsl,
                            start=False, stop=False))
                        mmend[pi] = p.op("tensor", lambda e, pi=pi, woff=woff, rhsh=rhsh, dc=dc: e.matmul(
                            PS[pi][:, :], wql[:, woff:woff + 128], rhsh,
                            start=False, stop=(dc == DCH - 1)), inc=MM)
                wtok[b] = mmend[5]
            for ocl in range(3):
                oc = og * 3 + ocl
                t_ev = [None, None]
                for th in range(2):
                    pi = ocl * 2 + th
                    sc_ = ISQD if oc < H else 1.0
                    t_ev[th] = p.op("scalar", lambda e, pi=pi, th=th, oc=oc, sc_=sc_: e.activation(
                        qf[:, th * 512: th * 512 + 512],
                        PS[pi][:, :], AF.Identity, bias=bqs[:, oc:oc + 1], scale=sc_),
                        waits=[mmend[pi], t_bq, qftok], inc=ACt)
                    pstok[pi] = t_ev[th]
                if oc < H:
                    t_hi = p.op("vector", lambda e: e.tensor_copy(qhib[:, :], qf[:, 0:1024]),
                                waits=[t_ev[0], t_ev[1], sgl], inc=VE)
                    t_lo = p.op("vector", lambda e: e.tensor_tensor(
                        qlob[:, :], qf[:, 0:1024], qhib[:, :], ALU.subtract), inc=VE)
                    qftok = t_lo
                    t_sqq = p.op("scalar", lambda e: e.square(sqq[:, :], qf[:, 0:1024]),
                                 waits=[t_ev[0], t_ev[1]], inc=ACt)
                    sgl4 = []
                    for t in range(2):
                        t_qn = p.op("tensor", lambda e, t=t: e.matmul(
                            PS[6][0:1, :], ones_cf[:, :], sqq[:, t * 512:(t + 1) * 512],
                            start=True, stop=True), waits=[t_sqq, pstok[6]], inc=MM)
                        t_qr = p.op("scalar", lambda e: e.copy(qrow[:, :], PS[6][0:1, :]),
                                    waits=[t_qn], inc=ACt)
                        pstok[6] = t_qr
                        sgl4.append(p.dma("gpsimd", d_qn[oc:oc + 1, t * 512:(t + 1) * 512], qrow[:, :],
                                          SG, waits=[t_qr]))
                        sgl4.append(p.dma("gpsimd", d_qsp[0, t, oc], qhib[:, t * 512:(t + 1) * 512],
                                          SG, waits=[t_hi]))
                        sgl4.append(p.dma("gpsimd", d_qsp[1, t, oc], qlob[:, t * 512:(t + 1) * 512],
                                          SG, waits=[t_lo]))
                    sgl = sgl4[-1]
                elif oc == H:
                    t_hi = p.op("vector", lambda e: e.tensor_copy(ownKh[:, :], qf[:, 0:1024]),
                                waits=[t_ev[0], t_ev[1]], inc=VE)
                    t_lo = p.op("vector", lambda e: e.tensor_tensor(
                        ownKl[:, :], qf[:, 0:1024], ownKh[:, :], ALU.subtract), inc=VE)
                    qftok = t_lo
                    p.dma("gpsimd", d_kvsh[0], ownKh[:, :], SG, waits=[t_hi])
                    sgl = p.dma("gpsimd", d_kvsh[1], ownKl[:, :], SG, waits=[t_lo])
                else:
                    tvc = None
                    for c in range(8):
                        pi = 6 + c % 2
                        t_tp = p.op("tensor", lambda e, c=c, pi=pi: e.transpose(
                            PS[pi][:, 0:128], qf[:, c * 128:(c + 1) * 128], ident[:, :]),
                            waits=[t_ev[0], t_ev[1], t_id, pstok[pi]], inc=MM)
                        tvc = p.op("scalar", lambda e, c=c, pi=pi: e.copy(
                            sqq[:, c * 128:(c + 1) * 128], PS[pi][:, 0:128]),
                            waits=[t_tp], inc=ACt)
                        pstok[pi] = tvc
                    qftok = tvc
                    t_hi = p.op("vector", lambda e: e.tensor_copy(ownVh[:, :], sqq[:, :]),
                                waits=[tvc], inc=VE)
                    t_lo = p.op("vector", lambda e: e.tensor_tensor(
                        ownVl[:, :], sqq[:, :], ownVh[:, :], ALU.subtract), inc=VE)
                    p.dma("gpsimd", d_kvsh[2], ownVh[:, :], SG, waits=[t_hi])
                    sgl = p.dma("gpsimd", d_kvsh[3], ownVl[:, :], SG, waits=[t_lo])
        if _DBG:
            p.dma("gpsimd", d_dbg_xn2[:, :], xnh2[:, :], SG, waits=[t_xn2b])
            p.dma("gpsimd", d_dbg_wqh[:, :], wqh[:, :], SG, waits=[(VE, VE.n)])
        t_gather = p.op("gpsimd",
                        lambda e: e.collective_compute(
                            "AllGather", ALU.bypass, replica_groups=[list(range(NC))],
                            ins=[d_kvsh[:, :, :]], outs=[d_kvall[:, :, :]]),
                        waits=[(SG, SG.n)], inc=CC)
    ps_qkv.close()
    p.barrier()
    if "qkv" in d_dbg:
        p.dma("gpsimd", d_dbg["qkv"][:, :], X[:, :], SG)

    ps_att = ExitStack()
    if _STOP_AFTER not in ("ffn1", "qkv"):
        Ta = lambda n, s, d: ps_att.enter_context(nc.sbuf_tensor(n, s, d))
        qhi = Ta("qhi", [128, H * 512], BF16)
        qlo = Ta("qlo", [128, H * 512], BF16)
        Ohi = Ta("Ohi", [128, H * 512], BF16)
        Olo = Ta("Olo", [128, H * 512], BF16)
        quad = Ta("quad", [128, 2 * 2048], BF16)
        Pf = Ta("Pf", [128, 2 * 512], F32)
        Phi = Ta("Phi", [128, 2 * 512], BF16)
        Plo = Ta("Plo", [128, 2 * 512], BF16)
        Bl = Ta("Bl", [128, 512], F32)
        Oft = Ta("Oft", [128, 512], F32)
        dm4 = Ta("dm4", [128, 4 * 512], BF16)
        padc = Ta("padc", [128, 192], F32)
        negst = Ta("negst", [1, 2 * 512], BF16)
        qnst = Ta("qnst", [1, 2 * 512], F32)
        qns2 = Ta("qns2", [1, 2 * 512], F32)
        lrow = Ta("lrow", [1, 2 * 512], F32)
        linv = Ta("linv", [1, 2 * 512], F32)
        wfcf = Ta("wfcf", [128, 2048], F32)
        wfh = Ta("wfh", [128, 2 * 2048], BF16)
        wfl = Ta("wfl", [128, 2 * 2048], BF16)
        tfc = Ta("tfc", [128, 512], F32)
        t_dm = p.dma("sync", dm4[:, :], d_dmask[:, :, :].transpose([1, 0, 2]), SY)
        t_pc = p.dma("sync", padc[:, :], d_padc[:, :], SY)
        t_k2 = None
        t_mx = None
        kcur = 0
        for r8 in range(NC):
            b = r8 % 2
            t_l = p.dma("gpsimd", quad[:, b * 1024:(b + 1) * 1024], d_kvall[4 * r8 + 0], SG,
                        waits=[t_gather, t_k2])
            t_k2 = p.op("vector", lambda e, b=b: e.tensor_tensor(
                Phi[:, :], quad[:, b * 1024:(b + 1) * 1024], quad[:, b * 1024:(b + 1) * 1024],
                ALU.mult), waits=[t_l], inc=VE)
            for hf in range(2):
                t_kn = p.op("tensor", lambda e, hf=hf: e.matmul(
                    PS[6][0:1, :], ones_cb[:, :], Phi[:, hf * 512:(hf + 1) * 512],
                    start=True, stop=True), waits=[t_k2, pstok[6]], inc=MM)
                t_rm = p.op("vector", lambda e: e.reduce_max(cells[:, 4:5], PS[6][0:1, :], AX.X),
                            waits=[t_kn] + ([t_mx] if t_mx else []), inc=VE)
                pstok[6] = t_rm
                t_mx = p.op("vector", lambda e, kcur=kcur: e.tensor_tensor(
                    cells[:, 1 - kcur:2 - kcur], cells[:, kcur:kcur + 1], cells[:, 4:5], ALU.max),
                    waits=[t_rm], inc=VE)
                kcur = 1 - kcur
        t_sqk = p.op("scalar", lambda e, kcur=kcur: e.activation(cells[:, 2:3], cells[:, kcur:kcur + 1],
                                                                 AF.Sqrt), waits=[t_mx], inc=ACt)
        t_nk = p.op("vector", lambda e: e.tensor_single_scalar(cells[:, 3:4], cells[:, 2:3],
                                                               -1.012, ALU.mult),
                    waits=[t_sqk], inc=VE)
        if _DBG:
            p.dma("gpsimd", d_dbg_kc[:, :], cells[:, :], SG, waits=[t_nk])
            p.dma("gpsimd", d_dbg_qn[:, :], d_qn[:, :], SG, waits=[t_nk])
            p.dma("gpsimd", d_dbg_kv[:, :], d_kvall[2], SG, waits=[t_nk])
            p.dma("gpsimd", d_dbg_ownk[:, :], ownKh[:, :], SG, waits=[t_nk])

        t_ODone = [None] * H
        t_pvlast = None
        t_fcw = None
        t_fcprev = None
        ucnt = 0
        quadtok = [t_k2, t_k2]
        ptok = [None, None]
        sptok = [None, None]
        negtok = [None, None]
        negcons = [None, None]
        for t in range(2):
            t_qh = p.dma("sync", qhi[:, :], d_qsp[0, t].transpose([1, 0, 2]), SY,
                         waits=[t_pvlast])
            t_ql = p.dma("sync", qlo[:, :], d_qsp[1, t].transpose([1, 0, 2]), SY)
            nq = RS[t] // 4
            for hp in range(H // 2):
                heads = (2 * hp, 2 * hp + 1)
                t_ld2 = None
                for hh, h in enumerate(heads):
                    t_ld2 = p.dma("gpsimd", qnst[:, hh * 512:(hh + 1) * 512],
                                  d_qn[h:h + 1, t * 512:(t + 1) * 512], SG,
                                  waits=[negtok[hh]])
                for hh, h in enumerate(heads):
                    t_s = p.op("scalar", lambda e, hh=hh: e.activation(
                        qns2[:, hh * 512:(hh + 1) * 512], qnst[:, hh * 512:(hh + 1) * 512],
                        AF.Sqrt), waits=[t_ld2], inc=ACt)
                    negtok[hh] = p.op("vector", lambda e, hh=hh: e.tensor_scalar(
                        negst[:, hh * 512:(hh + 1) * 512], qns2[:, hh * 512:(hh + 1) * 512],
                        cells[:, 3:4], None, ALU.mult), waits=[t_s, t_nk, negcons[hh]], inc=VE)
                if _DBG and t == 0 and hp == 0:
                    p.dma("gpsimd", d_dbg_neg[:, :], negst[:, :], SG, waits=[negtok[0], negtok[1]])
                Ot = [PS[0], PS[1]]
                lt = [PS[2], PS[3]]
                first = [True, True]
                t_lmm = [None, None]
                t_pv = [None, None]

                ud = []
                for qd in range(nq):
                    qb = qd % 2
                    q0 = qb * 2048
                    for hh, h in enumerate(heads):
                        for j in range(4):
                            u = 4 * qd + j
                            ud.append((hh, h,
                                       quad[:, q0 + j * 128: q0 + (j + 1) * 128],
                                       quad[:, q0 + 512 + j * 128: q0 + 512 + (j + 1) * 128],
                                       quad[:, q0 + 1024 + j * 128: q0 + 1024 + (j + 1) * 128],
                                       quad[:, q0 + 1536 + j * 128: q0 + 1536 + (j + 1) * 128],
                                       t * 96 + u, None,
                                       qd if (hh == 0 and j == 0) else None, qd))
                for hh, h in enumerate(heads):
                    for di in range(4):
                        c0 = t * 512 + di * 128
                        ud.append((hh, h,
                                   ownKh[:, c0:c0 + 128], ownKl[:, c0:c0 + 128],
                                   ownVh[:, c0:c0 + 128], ownVl[:, c0:c0 + 128],
                                   t * 96 + RS[t] + di, di, None, None))
                stage_tok = {}
                quad_last_pv = {}
                pend = {}

                def emit_S(u):
                    nonlocal ucnt
                    hh, h, kh_ap, kl_ap, vh_ap, vl_ap, bias_col, mask_di, sqd, qd = ud[u]
                    if sqd is not None:
                        qb = sqd % 2
                        rq = sqd if sqd < 8 else 15 - sqd
                        coff = 0 if sqd < 8 else 512
                        stage_tok[sqd] = p.dma(
                            "sync", quad[:, qb * 2048:(qb + 1) * 2048],
                            d_kvall[4 * rq:4 * rq + 4, :, coff:coff + 512].transpose([1, 0, 2]),
                            SY, waits=[quadtok[qb], t_gather])
                    t_stage = stage_tok.get(qd)
                    sb_ = 4 + ucnt % 2
                    pb = ucnt % 2
                    ucnt += 1
                    w0 = [pstok[sb_], t_qh, t_ql, negtok[hh], t_stage]
                    p.op("tensor", lambda e, sb_=sb_, kh_ap=kh_ap, h=h: e.matmul(
                        PS[sb_][:, :], kh_ap, qhi[:, h * 512:(h + 1) * 512],
                        start=True, stop=False), waits=w0)
                    p.op("tensor", lambda e, sb_=sb_, kh_ap=kh_ap, h=h: e.matmul(
                        PS[sb_][:, :], kh_ap, qlo[:, h * 512:(h + 1) * 512],
                        start=False, stop=False))
                    p.op("tensor", lambda e, sb_=sb_, kl_ap=kl_ap, h=h: e.matmul(
                        PS[sb_][:, :], kl_ap, qhi[:, h * 512:(h + 1) * 512],
                        start=False, stop=False))
                    t_S = p.op("tensor", lambda e, sb_=sb_, hh=hh: e.matmul(
                        PS[sb_][:, :], ones_rb[:, :], negst[:, hh * 512:(hh + 1) * 512],
                        start=False, stop=True), inc=MM)
                    negcons[hh] = t_S
                    if mask_di is not None:
                        t_S = p.op("vector", lambda e, sb_=sb_, mask_di=mask_di: e.tensor_tensor(
                            PS[sb_][:, :], PS[sb_][:, :], dm4[:, mask_di * 512:(mask_di + 1) * 512],
                            ALU.add), waits=[t_S, t_dm], inc=VE)
                    pend[u] = (sb_, pb, t_S)

                def emit_T(u):
                    nonlocal t_pvlast
                    hh, h, kh_ap, kl_ap, vh_ap, vl_ap, bias_col, mask_di, sqd, qd = ud[u]
                    sb_, pb, t_S = pend.pop(u)
                    t_e = p.op("scalar", lambda e, sb_=sb_, pb=pb, bias_col=bias_col: e.activation(
                        Pf[:, pb * 512:(pb + 1) * 512], PS[sb_][:, :], AF.Exp,
                        bias=padc[:, bias_col:bias_col + 1]),
                        waits=[t_S, sptok[pb], t_pc], inc=ACt)
                    pstok[sb_] = t_e
                    p.op("vector", lambda e, pb=pb: e.tensor_copy(
                        Phi[:, pb * 512:(pb + 1) * 512], Pf[:, pb * 512:(pb + 1) * 512]),
                        waits=[t_e, ptok[pb]])
                    t_sp = p.op("vector", lambda e, pb=pb: e.tensor_tensor(
                        Plo[:, pb * 512:(pb + 1) * 512], Pf[:, pb * 512:(pb + 1) * 512],
                        Phi[:, pb * 512:(pb + 1) * 512], ALU.subtract), inc=VE)
                    sptok[pb] = t_sp
                    st0 = first[hh]
                    w1 = [t_sp] + ([pstok[2 + hh], pstok[hh]] if st0 else [])
                    p.op("tensor", lambda e, hh=hh, pb=pb, st0=st0: e.matmul(
                        lt[hh][0:1, :], ones_cb[:, :], Phi[:, pb * 512:(pb + 1) * 512],
                        start=st0, stop=False), waits=w1)
                    t_lmm[hh] = p.op("tensor", lambda e, hh=hh, pb=pb: e.matmul(
                        lt[hh][0:1, :], ones_cb[:, :], Plo[:, pb * 512:(pb + 1) * 512],
                        start=False, stop=False), inc=MM)
                    p.op("tensor", lambda e, hh=hh, pb=pb, vh_ap=vh_ap, st0=st0: e.matmul(
                        Ot[hh][:, :], vh_ap, Phi[:, pb * 512:(pb + 1) * 512],
                        start=st0, stop=False))
                    p.op("tensor", lambda e, hh=hh, pb=pb, vh_ap=vh_ap: e.matmul(
                        Ot[hh][:, :], vh_ap, Plo[:, pb * 512:(pb + 1) * 512],
                        start=False, stop=False))
                    t_pv[hh] = p.op("tensor", lambda e, hh=hh, pb=pb, vl_ap=vl_ap: e.matmul(
                        Ot[hh][:, :], vl_ap, Phi[:, pb * 512:(pb + 1) * 512],
                        start=False, stop=False), inc=MM)
                    ptok[pb] = t_pv[hh]
                    first[hh] = False
                    t_pvlast = t_pv[hh]
                    if qd is not None and u % 8 == 7:
                        quadtok[qd % 2] = t_pv[hh]

                emit_S(0)
                for u in range(len(ud)):
                    if u + 1 < len(ud):
                        emit_S(u + 1)
                    emit_T(u)
                t_om = None
                for hh, h in enumerate(heads):
                    t_lr = p.op("scalar", lambda e, hh=hh: e.copy(
                        lrow[:, hh * 512:(hh + 1) * 512], lt[hh][0:1, :]),
                        waits=[t_lmm[hh], t_pv[hh]], inc=ACt)
                    if _DBG and t == 0 and hp == 0 and hh == 1:
                        p.dma("gpsimd", d_dbg_lrow[:, :], lrow[:, :], SG, waits=[t_lr])
                    pstok[2 + hh] = t_lr
                    t_li = p.op("vector", lambda e, hh=hh: e.reciprocal(
                        linv[:, hh * 512:(hh + 1) * 512], lrow[:, hh * 512:(hh + 1) * 512]),
                        waits=[t_lr], inc=VE)
                    pbk = 6 + hh
                    t_B = p.op("tensor", lambda e, hh=hh, pbk=pbk: e.matmul(
                        PS[pbk][:, :], ones_rf[:, :], linv[:, hh * 512:(hh + 1) * 512],
                        start=True, stop=True), waits=[t_li, pstok[pbk]], inc=MM)
                    t_Bc = p.op("scalar", lambda e, pbk=pbk: e.copy(
                        Bl[:, :], PS[pbk][:, :]), waits=[t_B, t_om], inc=ACt)
                    pstok[pbk] = t_Bc
                    t_om = p.op("vector", lambda e, hh=hh: e.tensor_tensor(
                        Oft[:, :], Ot[hh][:, :], Bl[:, :], ALU.mult), waits=[t_Bc], inc=VE)
                    pstok[hh] = t_om
                    p.op("vector", lambda e, h=h: e.tensor_copy(
                        Ohi[:, h * 512:(h + 1) * 512], Oft[:, :]),
                        waits=[t_fcprev])
                    t_ODone[h] = p.op("vector", lambda e, h=h: e.tensor_tensor(
                        Olo[:, h * 512:(h + 1) * 512], Oft[:, :],
                        Ohi[:, h * 512:(h + 1) * 512], ALU.subtract), inc=VE)
            if _DBG:
                p.dma("gpsimd", d_dbg_ohi[t], Ohi[:, :], SG, waits=[t_ODone[H - 1]])
                p.dma("gpsimd", d_dbg_olo[t], Olo[:, :], SG, waits=[t_ODone[H - 1]])
            t_fcmm_prev = None
            t_tfc_free = None
            for hg in range(8):
                heads = (2 * hg, 2 * hg + 1)
                t_ws = [None, None]
                for hh, h in enumerate(heads):
                    t_wl = p.dma("sync", wfcf[:, :], d_wfc[h * 128:(h + 1) * 128, :], SY,
                                 waits=[t_fcw, t_fcmm_prev])
                    p.op("vector", lambda e, hh=hh: e.tensor_copy(
                        wfh[:, hh * 2048:(hh + 1) * 2048], wfcf[:, :]),
                        waits=[t_wl, t_fcmm_prev])
                    t_ws[hh] = p.op("vector", lambda e, hh=hh: e.tensor_tensor(
                        wfl[:, hh * 2048:(hh + 1) * 2048], wfcf[:, :],
                        wfh[:, hh * 2048:(hh + 1) * 2048], ALU.subtract), inc=VE)
                    t_fcw = t_ws[hh]
                for ec in range(DCH):
                    pi = ec % 2
                    t_mmf = None
                    for hh, h in enumerate(heads):
                        w0 = [pstok[pi], t_ws[hh], t_ODone[h]] if hh == 0 else [t_ws[hh], t_ODone[h]]
                        p.op("tensor", lambda e, pi=pi, hh=hh, h=h, ec=ec: e.matmul(
                            PS[pi][:, :], wfh[:, hh * 2048 + ec * 128: hh * 2048 + (ec + 1) * 128],
                            Ohi[:, h * 512:(h + 1) * 512], start=(hh == 0), stop=False), waits=w0)
                        p.op("tensor", lambda e, pi=pi, hh=hh, h=h, ec=ec: e.matmul(
                            PS[pi][:, :], wfh[:, hh * 2048 + ec * 128: hh * 2048 + (ec + 1) * 128],
                            Olo[:, h * 512:(h + 1) * 512], start=False, stop=False))
                        t_mmf = p.op("tensor", lambda e, pi=pi, hh=hh, h=h, ec=ec: e.matmul(
                            PS[pi][:, :], wfl[:, hh * 2048 + ec * 128: hh * 2048 + (ec + 1) * 128],
                            Ohi[:, h * 512:(h + 1) * 512], start=False, stop=(hh == 1)), inc=MM)
                    xs = X[:, ec * TOK + t * RND: ec * TOK + t * RND + RND]
                    if hg == 0:
                        t_fc = p.op("scalar", lambda e, pi=pi, ec=ec: e.activation(
                            tfc[:, :], PS[pi][:, :], AF.Identity,
                            bias=bfc[:, ec:ec + 1]), waits=[t_mmf, t_bfc, t_tfc_free], inc=ACt)
                        t_fl = p.op("vector", lambda e, pi=pi, xs=xs: e.tensor_tensor(
                            xs, xs, tfc[:, :], ALU.add),
                            waits=[t_fc], inc=VE)
                        pstok[pi] = t_fc
                        t_tfc_free = t_fl
                    else:
                        t_fl = p.op("vector", lambda e, pi=pi, xs=xs: e.tensor_tensor(
                            xs, xs, PS[pi][:, :], ALU.add), waits=[t_mmf], inc=VE)
                        pstok[pi] = t_fl
                    t_fcmm_prev = t_mmf
            t_fcprev = t_fcmm_prev
    ps_att.close()
    p.barrier()
    if "attn" in d_dbg:
        p.dma("gpsimd", d_dbg["attn"][:, :], X[:, :], SG)

    if _STOP_AFTER is None:
        ffn_phase("ffn2")
        if "ffn2" in d_dbg:
            p.dma("gpsimd", d_dbg["ffn2"][:, :], X[:, :], SG)

    with nc.sbuf_tensor("orow", [128, 2 * D], F32) as orow:
        otok = [None, None]
        for r in range(2):
            for tq in range(4):
                b = tq % 2
                t_cp = None
                for dc in range(DCH):
                    pi = 6 + dc % 2
                    t_tp = p.op("tensor", lambda e, r=r, tq=tq, dc=dc, pi=pi: e.transpose(
                        PS[pi][:, 0:128],
                        X[:, dc * TOK + r * RND + tq * 128: dc * TOK + r * RND + (tq + 1) * 128],
                        ident[:, :]), waits=[pstok[pi], otok[b]], inc=MM)
                    t_cp = p.op("scalar", lambda e, b=b, dc=dc, pi=pi: e.copy(
                        orow[:, b * D + dc * 128: b * D + (dc + 1) * 128], PS[pi][:, 0:128]),
                        waits=[t_tp], inc=ACt)
                    pstok[pi] = t_cp
                otok[b] = p.dma("gpsimd", d_out[r * 512 + tq * 128: r * 512 + (tq + 1) * 128, :],
                                orow[:, b * D:(b + 1) * D], SG, waits=[t_cp])
    p.wait("gpsimd", [(SG, SG.n), (SY, SY.n), (SC, SC.n)])

    with es, nc.Block() as block:
        p.build(block)
    return nc, p.nops


def _host_inputs(L0_attn_x, L0_attn_w_qkv, L0_attn_b_qkv, L0_attn_w_fc, L0_attn_b_fc,
                 L0_ffn_W_up, L0_ffn_W_gate, L0_ffn_W_down):
    x = np.ascontiguousarray(np.asarray(L0_attn_x, dtype=np.float32)[0])
    wqkv = np.ascontiguousarray(np.asarray(L0_attn_w_qkv, dtype=np.float32))
    bqkv = np.asarray(L0_attn_b_qkv, dtype=np.float32)
    wfc = np.ascontiguousarray(np.asarray(L0_attn_w_fc, dtype=np.float32))
    bfc_ = np.asarray(L0_attn_b_fc, dtype=np.float32)
    wu = np.ascontiguousarray(np.asarray(L0_ffn_W_up, dtype=np.float32))
    wg = np.ascontiguousarray(np.asarray(L0_ffn_W_gate, dtype=np.float32))
    wd = np.ascontiguousarray(np.asarray(L0_ffn_W_down, dtype=np.float32))

    bqkv2d = np.ascontiguousarray(bqkv.reshape(OCH, 128).T)
    bfc2d = np.ascontiguousarray(bfc_.reshape(DCH, 128).T)
    ident = np.eye(128, dtype=np.float32)
    identb = np.eye(128).astype(BF)
    mrange = np.arange(128)[:, None]
    jrange = np.arange(RND)[None, :]
    dmask = np.stack([np.where(128 * oi + mrange > jrange, np.float32(NEG), np.float32(0.0))
                      for oi in range(4)]).astype(BF)

    in_maps = []
    for i in range(NC):
        rows = np.concatenate([x[512 * i:512 * (i + 1)], x[512 * (15 - i):512 * (16 - i)]], axis=0)
        padc = np.zeros((128, 192), dtype=np.float32)
        for t, blk in ((0, i), (1, 15 - i)):
            nreal = 4 * blk
            for u in range(RS[t]):
                if u >= nreal:
                    padc[:, t * 96 + u] = NEG
        im = {
            "x": np.ascontiguousarray(rows),
            "wqkv": wqkv, "bqkv2d": bqkv2d, "wfc": wfc, "bfc2d": bfc2d,
            "wu": wu, "wg": wg, "wd": wd,
            "wu_sh": np.ascontiguousarray(wu[1024 * i:1024 * (i + 1)]),
            "wg_sh": np.ascontiguousarray(wg[1024 * i:1024 * (i + 1)]),
            "wd_sh": np.ascontiguousarray(wd[256 * i:256 * (i + 1)]),
            "ident": ident, "identb": identb, "dmask": dmask,
            "padcol": padc,
        }
        in_maps.append(im)
    return in_maps


def kernel(**inputs):
    global _last_results
    in_maps = _host_inputs(**inputs)
    nc, nops = build_nc()
    res = run_bass_kernel_spmd(nc, in_maps, core_ids=list(range(NC)))
    _last_results = res
    out = np.empty((SEQ, D), dtype=np.float32)
    for i in range(NC):
        xo = np.asarray(res.results[i]["xout"])
        out[512 * i:512 * (i + 1)] = xo[:512]
        out[512 * (15 - i):512 * (16 - i)] = xo[512:]
    return out[None, :, :]


# version 07 (final)
# speedup vs baseline: 1.0883x; 1.0883x over previous
import math
import numpy as np
import ml_dtypes
import concourse.bass as bass
import concourse.mybir as mybir
from concourse.bass_utils import run_bass_kernel_spmd
from contextlib import ExitStack

F32 = mybir.dt.float32
BF16 = mybir.dt.bfloat16
AF = mybir.ActivationFunctionType
ALU = mybir.AluOpType
AX = mybir.AxisListType
BF = ml_dtypes.bfloat16

NC = 8
D = 2048
DCH = 16
SEQ = 8192
TOK = 1024
RND = 512
E = 8192
H = 16
QKV = 2304
OCH = 18
RS = (28, 60)
NEG = -1.0e38
ISQD = 1.0 / math.sqrt(128.0)

ENGS = ("sync", "tensor", "vector", "scalar", "gpsimd")

_DBG = []
_STOP_AFTER = None
_last_results = None


class _Sem:
    __slots__ = ("h", "n", "name")

    def __init__(self, h, name):
        self.h, self.n, self.name = h, 0, name


class _Prog:
    def __init__(self, nc, es):
        self.nc, self.es = nc, es
        self.q = {e: [] for e in ENGS}
        self._lw = {e: {} for e in ENGS}
        self.nops = 0
        self.nbar = 0

    def sem(self, name):
        return _Sem(self.es.enter_context(self.nc.semaphore(name)), name)

    def op(self, eng, fn, waits=(), inc=None, k=1):
        lw = self._lw[eng]
        eff = []
        for t in waits:
            if t is None:
                continue
            s, v = t
            if v is None or lw.get(s.name, -1) >= v:
                continue
            eff.append((s, v))
            lw[s.name] = v
        tgt = None
        if inc is not None:
            inc.n += k
            tgt = inc.n

        def cl(e, fn=fn, eff=tuple(eff), inc=inc, k=k):
            for s, v in eff:
                e.wait_ge(s.h, v)
            r = fn(e)
            if inc is not None:
                r.then_inc(inc.h, k)

        self.q[eng].append(cl)
        self.nops += 1
        return (inc, tgt) if inc is not None else None

    def wait(self, eng, waits):
        self.op(eng, lambda e: e.nop(), waits=waits)

    def dma(self, q, out, in_, sem, waits=(), transpose=False):
        return self.op(
            q,
            lambda e, out=out, in_=in_, tr=transpose: e.dma_start(out=out, in_=in_, transpose=tr),
            waits=waits, inc=sem, k=16)

    def barrier(self):
        self.nbar += 1
        tgt = 5 * self.nbar
        for eng in ENGS:
            w = [(s, s.n) for s in self.dma_sems.get(eng, ())]
            self.op(eng, lambda e: e.nop(), waits=w, inc=self.BAR, k=1)
        for eng in ENGS:
            self.wait(eng, [(self.BAR, tgt)])

    def build(self, block):
        for eng in ENGS:
            ops = self.q[eng]
            if not ops:
                continue

            def run(e, ops=ops):
                for cl in ops:
                    cl(e)

            getattr(block, eng)(run)


def build_nc():
    nc = bass.Bass("TRN2", target_bir_lowering=False)

    d_x = nc.dram_tensor("x", [TOK, D], F32, kind="ExternalInput")
    d_wqkv = nc.dram_tensor("wqkv", [D, QKV], F32, kind="ExternalInput")
    d_bqkv = nc.dram_tensor("bqkv2d", [128, OCH], F32, kind="ExternalInput")
    d_wfc = nc.dram_tensor("wfc", [D, D], F32, kind="ExternalInput")
    d_bfc = nc.dram_tensor("bfc2d", [128, DCH], F32, kind="ExternalInput")
    d_wu = nc.dram_tensor("wu", [E, D], F32, kind="ExternalInput")
    d_wg = nc.dram_tensor("wg", [E, D], F32, kind="ExternalInput")
    d_wd = nc.dram_tensor("wd", [D, E], F32, kind="ExternalInput")
    d_wu_sh = nc.dram_tensor("wu_sh", [E // NC, D], F32, kind="ExternalInput")
    d_wg_sh = nc.dram_tensor("wg_sh", [E // NC, D], F32, kind="ExternalInput")
    d_wd_sh = nc.dram_tensor("wd_sh", [D // NC, E], F32, kind="ExternalInput")
    d_ident = nc.dram_tensor("ident", [128, 128], F32, kind="ExternalInput")
    d_identb = nc.dram_tensor("identb", [128, 128], BF16, kind="ExternalInput")
    d_dmask = nc.dram_tensor("dmask", [4, 128, RND], BF16, kind="ExternalInput")
    d_padc = nc.dram_tensor("padcol", [128, 192], F32, kind="ExternalInput")
    d_out = nc.dram_tensor("xout", [TOK, D], F32, kind="ExternalOutput")
    d_dbg = {t: nc.dram_tensor("dbg_" + t, [128, DCH * TOK], F32, kind="ExternalOutput")
             for t in _DBG}
    if _DBG:
        d_dbg_tug = nc.dram_tensor("dbg_tug", [2048, D], BF16, kind="ExternalOutput")
        d_dbg_td = nc.dram_tensor("dbg_td", [D, 1024], BF16, kind="ExternalOutput")
        d_dbg_ssc = nc.dram_tensor("dbg_ssc", [128, 10], F32, kind="ExternalOutput")
        d_dbg_ars = nc.dram_tensor("dbg_ars", [1, 24], F32, kind="ExternalOutput")
        d_dbg_cells = nc.dram_tensor("dbg_cells", [1, 16], F32, kind="ExternalOutput")
        d_dbg_throw = nc.dram_tensor("dbg_throw", [1, 10], F32, kind="ExternalOutput")
        d_dbg_kc = nc.dram_tensor("dbg_kc", [1, 16], F32, kind="ExternalOutput")
        d_dbg_neg = nc.dram_tensor("dbg_neg", [1, 1024], BF16, kind="ExternalOutput")
        d_dbg_lrow = nc.dram_tensor("dbg_lrow", [1, 1024], F32, kind="ExternalOutput")
        d_dbg_qn = nc.dram_tensor("dbg_qn", [H, TOK], F32, kind="ExternalOutput")
        d_dbg_kv = nc.dram_tensor("dbg_kv", [128, TOK], BF16, kind="ExternalOutput")
        d_dbg_ownk = nc.dram_tensor("dbg_ownk", [128, TOK], BF16, kind="ExternalOutput")
        d_dbg_xn2 = nc.dram_tensor("dbg_xn2", [128, DCH * TOK], BF16, kind="ExternalOutput")
        d_dbg_ohi = nc.dram_tensor("dbg_ohi", [2, 128, H * 512], BF16, kind="ExternalOutput")
        d_dbg_olo = nc.dram_tensor("dbg_olo", [2, 128, H * 512], BF16, kind="ExternalOutput")
        d_dbg_ownv = nc.dram_tensor("dbg_ownv", [128, TOK], BF16, kind="ExternalOutput")
        d_dbg_wqh = nc.dram_tensor("dbg_wqh", [128, 2 * 384], BF16, kind="ExternalOutput")

    d_tug = nc.dram_tensor("tug_nat", [2 * E, D], BF16)
    d_td = nc.dram_tensor("td_nat", [D, E], BF16)
    d_arin = nc.dram_tensor("ar_in", [3, 8], F32)
    d_arout = nc.dram_tensor("ar_out", [3, 8], F32)
    d_kvsh = nc.dram_tensor("kv_sh", [4, 128, TOK], BF16)
    d_kvall = nc.dram_tensor("kv_all", [4 * NC, 128, TOK], BF16)
    d_qn = nc.dram_tensor("qn_dram", [H, TOK], F32)
    d_qsp = nc.dram_tensor("qspill", [2, 2, H, 128, RND], BF16)

    es = ExitStack()
    p = _Prog(nc, es)
    SY = p.sem("sy"); SC = p.sem("sc"); SG = p.sem("sg")
    QW = [p.sem(f"qw{i}") for i in range(4)]
    MM = p.sem("mm"); VE = p.sem("ve"); ACt = p.sem("ac"); CC = p.sem("cc")
    p.BAR = p.sem("bar")
    p.dma_sems = {"sync": (SY,), "scalar": (SC,), "gpsimd": (SG,)}

    sb = lambda n, s, d: es.enter_context(nc.sbuf_tensor(n, s, d))
    X = sb("X", [128, DCH * TOK], F32)
    ones_cf = sb("ones_cf", [128, 1], F32)
    ones_rf = sb("ones_rf", [1, 128], F32)
    ones_cb = sb("ones_cb", [128, 1], BF16)
    ones_rb = sb("ones_rb", [1, 128], BF16)
    ident = sb("s_ident", [128, 128], F32)
    identb = sb("s_identb", [128, 128], BF16)
    ssc = sb("ssc", [128, 10], F32)
    bqs = sb("bqs", [128, OCH], F32)
    bfc = sb("bfc", [128, DCH], F32)
    ownKh = sb("ownKh", [128, TOK], BF16)
    ownKl = sb("ownKl", [128, TOK], BF16)
    ownVh = sb("ownVh", [128, TOK], BF16)
    ownVl = sb("ownVl", [128, TOK], BF16)
    cells = sb("cells", [1, 16], F32)
    throw = sb("throw", [1, 10], F32)

    PS = [es.enter_context(nc.psum_tensor(f"ps{i}", [128, 512], F32)) for i in range(8)]
    pstok = [None] * 8

    p.op("vector", lambda e: e.memset(ones_cf[:, :], 1.0))
    p.op("vector", lambda e: e.memset(ones_rf[:, :], 1.0))
    p.op("vector", lambda e: e.memset(ones_cb[:, :], 1.0))
    p.op("vector", lambda e: e.memset(cells[:, :], 0.0))
    t_ones = p.op("vector", lambda e: e.memset(ones_rb[:, :], 1.0), inc=VE)
    t_id = p.dma("sync", ident[:, :], d_ident[:, :], SY)
    t_idb = p.dma("sync", identb[:, :], d_identb[:, :], SY)
    t_bq0 = p.dma("sync", bqs[:, :], d_bqkv[:, :], SY)
    t_bfc = p.dma("sync", bfc[:, :], d_bfc[:, :], SY)
    t_bq = p.op("scalar", lambda e: e.activation(bqs[:, 0:H], bqs[:, 0:H], AF.Identity, scale=ISQD),
                waits=[t_bq0], inc=ACt)

    with (nc.sbuf_tensor("xrow", [128, 2 * D], F32) as xrow,
          nc.sbuf_tensor("ashard", [128, E], F32) as ash,
          nc.sbuf_tensor("racc", [128, 20], F32) as racc,
          nc.sbuf_tensor("acc3", [128, 3], F32) as acc3,
          nc.sbuf_tensor("ones8", [128, 8], F32) as ones8,
          nc.sbuf_tensor("pa", [3, 8], F32) as pa,
          nc.sbuf_tensor("arsrow", [1, 24], F32) as arsrow):
        rowtok = [None, None]
        for rc in range(8):
            b = rc % 2
            t_l = p.dma("sync", xrow[:, b * D:(b + 1) * D], d_x[rc * 128:(rc + 1) * 128, :], SY,
                        waits=[rowtok[b]])
            for dc in range(DCH):
                psi = 6 + (dc % 2)
                t_tp = p.op("tensor",
                            lambda e, b=b, dc=dc, psi=psi: e.transpose(
                                PS[psi][:, 0:128],
                                xrow[:, b * D + dc * 128: b * D + (dc + 1) * 128], ident[:, :]),
                            waits=[t_l, t_id, pstok[psi]], inc=MM)
                t_ev = p.op("scalar",
                            lambda e, rc=rc, dc=dc, psi=psi: e.copy(
                                X[:, dc * TOK + rc * 128: dc * TOK + (rc + 1) * 128],
                                PS[psi][:, 0:128]),
                            waits=[t_tp], inc=ACt)
                pstok[psi] = t_ev
                if dc == DCH - 1:
                    rowtok[b] = t_ev
        t_X0 = (ACt, ACt.n)

        p.op("vector", lambda e: e.memset(ones8[:, :], 1.0))
        shtok = [None, None]
        col = 0
        t_r = None
        tiles_a = []
        for k in range(8):
            tiles_a.append(d_wu_sh[k * 128:(k + 1) * 128, :])
        for k in range(8):
            tiles_a.append(d_wg_sh[k * 128:(k + 1) * 128, :])
        for k in range(2):
            for cs in range(2):
                tiles_a.append(d_wd_sh[k * 128:(k + 1) * 128, cs * 4096:(cs + 1) * 4096])
        for src in tiles_a:
            b = col % 2
            fd = src.shape[1]
            t_l = p.dma("scalar", ash[:, b * 4096:b * 4096 + fd], src, SC, waits=[shtok[b]])
            t_r = p.op("vector",
                       lambda e, b=b, col=col, fd=fd: e.reduce_sum(
                           racc[:, col:col + 1], ash[:, b * 4096:b * 4096 + fd],
                           AX.X, apply_absolute_value=True),
                       waits=[t_l], inc=VE)
            shtok[b] = t_r
            col += 1
        p.op("vector", lambda e: e.reduce_sum(acc3[:, 0:1], racc[:, 0:8], AX.X), waits=[t_r])
        p.op("vector", lambda e: e.reduce_sum(acc3[:, 1:2], racc[:, 8:16], AX.X))
        t_ac = p.op("vector", lambda e: e.reduce_sum(acc3[:, 2:3], racc[:, 16:20], AX.X), inc=VE)
        t_pa = p.op("tensor", lambda e: e.matmul(PS[0][0:3, 0:8], acc3[:, :], ones8[:, :],
                                                 start=True, stop=True),
                    waits=[t_ac, pstok[0]], inc=MM)
        t_pac = p.op("scalar", lambda e: e.copy(pa[:, :], PS[0][0:3, 0:8]), waits=[t_pa], inc=ACt)
        pstok[0] = t_pac
        t_pad = p.dma("gpsimd", d_arin[:, :], pa[:, :], SG, waits=[t_pac])
        t_ar = p.op("gpsimd",
                    lambda e: e.collective_compute(
                        "AllReduce", ALU.add, replica_groups=[list(range(NC))],
                        ins=[d_arin[:, :]], outs=[d_arout[:, :]]),
                    waits=[t_pad], inc=CC)
        t_arl = p.dma("gpsimd", arsrow[:, :], d_arout[:, :], SG, waits=[t_ar])
        C = 1.0 / (1 << 24)
        tk = t_arl
        steps = [
            lambda e: e.tensor_single_scalar(cells[:, 4:5], arsrow[:, 0:1], C, ALU.mult),
            lambda e: e.tensor_single_scalar(cells[:, 4:5], cells[:, 4:5], 1e-8, ALU.add),
            lambda e: e.tensor_single_scalar(cells[:, 5:6], arsrow[:, 8:9], C, ALU.mult),
            lambda e: e.tensor_single_scalar(cells[:, 5:6], cells[:, 5:6], 1e-8, ALU.add),
            lambda e: e.tensor_single_scalar(cells[:, 6:7], arsrow[:, 16:17], C, ALU.mult),
            lambda e: e.tensor_single_scalar(cells[:, 6:7], cells[:, 6:7], 1e-8, ALU.add),
            lambda e: e.tensor_single_scalar(throw[:, 0:1], cells[:, 4:5], 0.5, ALU.mult),
            lambda e: e.tensor_single_scalar(throw[:, 1:2], cells[:, 5:6], 0.5, ALU.mult),
            lambda e: e.tensor_single_scalar(throw[:, 7:8], cells[:, 6:7], 0.5, ALU.mult),
            lambda e: e.tensor_single_scalar(throw[:, 2:3], throw[:, 7:8], 0.5, ALU.mult),
            lambda e: e.tensor_copy(throw[:, 3:4], throw[:, 0:1]),
            lambda e: e.tensor_single_scalar(throw[:, 4:5], throw[:, 0:1], -1.0, ALU.mult),
            lambda e: e.tensor_copy(throw[:, 5:6], throw[:, 1:2]),
            lambda e: e.tensor_single_scalar(throw[:, 6:7], throw[:, 1:2], -1.0, ALU.mult),
            lambda e: e.tensor_single_scalar(throw[:, 8:9], throw[:, 7:8], -1.0, ALU.mult),
        ]
        for fn in steps:
            tk = p.op("vector", fn, waits=[tk], inc=VE)
        t_th = tk
        t_bc = p.op("tensor", lambda e: e.matmul(PS[1][:, 0:10], ones_rf[:, :], throw[:, :],
                                                 start=True, stop=True),
                    waits=[t_th, t_ones, pstok[1]], inc=MM)
        t_ssc = p.op("scalar", lambda e: e.copy(ssc[:, :], PS[1][:, 0:10]), waits=[t_bc], inc=ACt)
        pstok[1] = t_ssc
        if _DBG:
            p.dma("gpsimd", d_dbg_ssc[:, :], ssc[:, :], SG, waits=[t_ssc])
            p.dma("gpsimd", d_dbg_ars[:, :], arsrow[:, :], SG, waits=[t_arl])
            p.dma("gpsimd", d_dbg_cells[:, :], cells[:, :], SG, waits=[t_th])
            p.dma("gpsimd", d_dbg_throw[:, :], throw[:, :], SG, waits=[t_th])
    p.barrier()

    qstate = dict(ldtok=[None, None], ttok=[None, None], t_sg=None, qi=0)

    def quant_tiles(tiles, qbufs, qsem):
        wld, s1b, s2b, tb = qbufs
        tok = None
        for src_ap, dst_ap, thc, nthc in tiles:
            t_l = p.dma("gpsimd", wld[:, :], src_ap.transpose([1, 0, 2]),
                        SG, waits=[qstate["ldtok"][0]])
            t_s1 = p.op("scalar", lambda e, nthc=nthc: e.sign(
                s1b[:, :], wld[:, :], bias=ssc[:, nthc:nthc + 1]),
                waits=[t_l, t_ssc, qstate["t_sg"]], inc=ACt)
            t_s2 = p.op("scalar", lambda e, thc=thc: e.sign(
                s2b[:, :], wld[:, :], bias=ssc[:, thc:thc + 1]),
                waits=[], inc=ACt)
            qstate["ldtok"][0] = t_s2
            t_t = p.op("vector", lambda e: e.tensor_tensor(
                tb[:, :], s1b[:, :], s2b[:, :], ALU.add),
                waits=[t_s2, qstate["ttok"][0]], inc=VE)
            qstate["t_sg"] = t_t
            tok = p.dma("gpsimd", dst_ap.transpose([1, 0, 2]), tb[:, :],
                        qsem, waits=[t_t])
            qstate["ttok"][0] = tok
            qstate["qi"] += 1
        return [tok]

    def tug_group_tiles(g):
        tiles = []
        for rp in range(2 * g, 2 * g + 2):
            for ch in range(2):
                tiles.append((
                    d_wu[rp * 256:(rp + 1) * 256, ch * 1024:(ch + 1) * 1024]
                    .rearrange('(a b) c -> a b c', b=128),
                    d_tug[512 * rp:512 * rp + 256, ch * 1024:(ch + 1) * 1024]
                    .rearrange('(a b) c -> a b c', b=128), 3, 4))
        for rp in range(2 * g, 2 * g + 2):
            for ch in range(2):
                tiles.append((
                    d_wg[rp * 256:(rp + 1) * 256, ch * 1024:(ch + 1) * 1024]
                    .rearrange('(a b) c -> a b c', b=128),
                    d_tug[512 * rp + 256:512 * rp + 512, ch * 1024:(ch + 1) * 1024]
                    .rearrange('(a b) c -> a b c', b=128), 5, 6))
        return tiles

    def td_cs_tiles(g):
        if g % 2 == 1:
            return []
        cs = g // 2
        return [(d_wd[rp * 256:(rp + 1) * 256, cs * 1024:(cs + 1) * 1024]
                 .rearrange('(a b) c -> a b c', b=128),
                 d_td[rp * 256:(rp + 1) * 256, cs * 1024:(cs + 1) * 1024]
                 .rearrange('(a b) c -> a b c', b=128), 7, 8)
                for rp in range(D // 256)]

    def ln_emit(xnh, xnl, r, coloff, bufs, t_extra=()):
        sq, mrow, vrow, srow, rrow, trow, wrow, xt1, xt2 = (
            bufs["sq"], bufs["mrow"], bufs["vrow"], bufs["srow"], bufs["rrow"],
            bufs["trow"], bufs["wrow"], bufs["xt1"], bufs["xt2"])
        sqtok = [None]
        t_s2m = None
        for dc in range(DCH):
            xs = X[:, dc * TOK + r * RND: dc * TOK + r * RND + RND]
            t_sq = p.op("scalar", lambda e, xs=xs: e.square(sq[:, :], xs),
                        waits=[sqtok[0], t_X0] + list(t_extra), inc=ACt)
            p.op("tensor", lambda e, dc=dc, xs=xs: e.matmul(
                PS[6][0:1, :], ones_cf[:, :], xs, start=(dc == 0), stop=(dc == DCH - 1)),
                waits=[pstok[6], t_X0] if dc == 0 else [])
            t_s2m = p.op("tensor", lambda e, dc=dc: e.matmul(
                PS[7][0:1, :], ones_cf[:, :], sq[:, :],
                start=(dc == 0), stop=(dc == DCH - 1)),
                waits=([pstok[7]] if dc == 0 else []) + [t_sq], inc=MM)
            sqtok[0] = t_s2m
        IC = 1.0 / 2048.0
        tk = p.op("vector", lambda e: e.tensor_single_scalar(mrow[:, :], PS[6][0:1, :], IC, ALU.mult),
                  waits=[t_s2m], inc=VE)
        tk = p.op("vector", lambda e: e.tensor_single_scalar(vrow[:, :], PS[7][0:1, :], IC, ALU.mult),
                  waits=[tk], inc=VE)
        tk = p.op("vector", lambda e: e.tensor_tensor(trow[:, :], mrow[:, :], mrow[:, :], ALU.mult),
                  waits=[tk], inc=VE)
        tk = p.op("vector", lambda e: e.tensor_tensor(wrow[:, :], vrow[:, :], trow[:, :], ALU.subtract),
                  waits=[tk], inc=VE)
        t_ve = p.op("vector", lambda e: e.tensor_single_scalar(vrow[:, :], wrow[:, :], 1e-5, ALU.add),
                    waits=[tk], inc=VE)
        t_prev = p.op("scalar", lambda e: e.activation(srow[:, :], vrow[:, :], AF.Sqrt), waits=[t_ve], inc=ACt)
        for _ in range(2):
            t_prev = p.op("vector", lambda e: e.reciprocal(rrow[:, :], srow[:, :]), waits=[t_prev], inc=VE)
            t_prev = p.op("vector", lambda e: e.tensor_tensor(trow[:, :], vrow[:, :], rrow[:, :], ALU.mult),
                          waits=[t_prev], inc=VE)
            t_prev = p.op("vector", lambda e: e.tensor_tensor(wrow[:, :], trow[:, :], srow[:, :], ALU.add),
                          waits=[t_prev], inc=VE)
            t_prev = p.op("vector", lambda e: e.tensor_single_scalar(srow[:, :], wrow[:, :], 0.5, ALU.mult),
                          waits=[t_prev], inc=VE)
        t_prev = p.op("vector", lambda e: e.reciprocal(rrow[:, :], srow[:, :]), waits=[t_prev], inc=VE)
        t_mr = p.op("vector", lambda e: e.tensor_tensor(trow[:, :], mrow[:, :], rrow[:, :], ALU.mult),
                    waits=[t_prev], inc=VE)
        t_b1 = p.op("tensor", lambda e: e.matmul(PS[6][:, :], ones_rf[:, :], rrow[:, :],
                                                 start=True, stop=True), waits=[t_mr, pstok[6]], inc=MM)
        t_b2 = p.op("tensor", lambda e: e.matmul(PS[7][:, :], ones_rf[:, :], trow[:, :],
                                                 start=True, stop=True), waits=[t_mr, pstok[7]], inc=MM)
        t_xn = None
        for dc in range(DCH):
            xs = X[:, dc * TOK + r * RND: dc * TOK + r * RND + RND]
            c0 = coloff(dc)
            t_a1 = p.op("vector", lambda e, xs=xs: e.tensor_tensor(xt1[:, :], xs, PS[6][:, :], ALU.mult),
                        waits=[t_b1, t_b2])
            t_a2 = p.op("vector", lambda e: e.tensor_tensor(xt2[:, :], xt1[:, :], PS[7][:, :], ALU.subtract))
            p.op("vector", lambda e, c0=c0: e.tensor_copy(xnh[:, c0:c0 + RND], xt2[:, :]),
                 waits=list(t_extra))
            t_xn = p.op("vector", lambda e, c0=c0: e.tensor_tensor(
                xnl[:, c0:c0 + RND], xt2[:, :], xnh[:, c0:c0 + RND], ALU.subtract), inc=VE)
            pstok[6] = t_xn
            pstok[7] = t_xn
        return t_xn

    def ffn_emit(r, xnh, xnl, t_xn, bufs, quant_cb=None, qug=None, qtd=None):
        ugb, tdb, silub, hfb, hgh, hgl = (bufs["ugb"], bufs["tdb"], bufs["silub"],
                                          bufs["hfb"], bufs["hgh"], bufs["hgl"])
        ugtok = bufs["ugtok"]; tdtok = bufs["tdtok"]; htok = bufs["htok"]
        t_ugfinal = None
        for g in range(16):
            if quant_cb is not None:
                quant_cb(g)
            tdld = []
            for ecl in range(4):
                ec = 4 * g + ecl
                tdld.append(p.dma("scalar", tdb[:, ecl * 2048:(ecl + 1) * 2048],
                                  d_td[0:D, ec * 128:(ec + 1) * 128], SC,
                                  waits=[tdtok[0]] + (qtd[g] if qtd else []), transpose=True))
            t_hready = None
            for s2 in range(2):
                ss = 2 * g + s2
                b = ss % 2
                t_ug = None
                for dc in range(DCH):
                    t_ug = p.dma("sync", ugb[b][:, dc * 512:(dc + 1) * 512],
                                 d_tug[512 * ss:512 * ss + 512, dc * 128:(dc + 1) * 128], SY,
                                 waits=[ugtok[b]] + (qug[g] if qug else []), transpose=True)
                mmt = [None] * 4
                for dc in range(DCH):
                    for pi, off in ((0, 0), (1, 128), (2, 256), (3, 384)):
                        w0 = [pstok[pi], t_ug, t_xn] if dc == 0 else []
                        p.op("tensor", lambda e, pi=pi, off=off, dc=dc, b=b: e.matmul(
                            PS[pi][:, :], ugb[b][:, dc * 512 + off: dc * 512 + off + 128],
                            xnh[:, dc * 512:(dc + 1) * 512],
                            start=(dc == 0), stop=False), waits=w0)
                        mmt[pi] = p.op("tensor", lambda e, pi=pi, off=off, dc=dc, b=b: e.matmul(
                            PS[pi][:, :], ugb[b][:, dc * 512 + off: dc * 512 + off + 128],
                            xnl[:, dc * 512:(dc + 1) * 512],
                            start=False, stop=(dc == DCH - 1)), inc=MM)
                ugtok[b] = mmt[3]
                t_ugfinal = mmt[3]
                for ecl in range(2):
                    hsl = 2 * s2 + ecl
                    pu, pg = ecl, 2 + ecl
                    bb = hsl % 2
                    t_si = p.op("scalar", lambda e, pu=pu, bb=bb: e.activation(
                        silub[:, bb * 512:(bb + 1) * 512], PS[pu][:, :], AF.Silu,
                        scale=ssc[:, 0:1]), waits=[mmt[pu], bufs["silutok"][bb], t_ssc], inc=ACt)
                    pstok[pu] = t_si
                    t_hf = p.op("vector", lambda e, pg=pg, bb=bb: e.scalar_tensor_tensor(
                        hfb[:, bb * 512:(bb + 1) * 512], PS[pg][:, :], ssc[:, 1:2],
                        silub[:, bb * 512:(bb + 1) * 512], ALU.mult, ALU.mult),
                        waits=[mmt[pg], t_si, bufs["hftok"][bb]], inc=VE)
                    pstok[pg] = t_hf
                    bufs["silutok"][bb] = t_hf
                    p.op("vector", lambda e, hsl=hsl, bb=bb: e.tensor_copy(
                        hgh[:, hsl * 512:(hsl + 1) * 512], hfb[:, bb * 512:(bb + 1) * 512]),
                        waits=[htok[0]])
                    t_hl = p.op("vector", lambda e, hsl=hsl, bb=bb: e.tensor_tensor(
                        hgl[:, hsl * 512:(hsl + 1) * 512], hfb[:, bb * 512:(bb + 1) * 512],
                        hgh[:, hsl * 512:(hsl + 1) * 512], ALU.subtract), inc=VE)
                    bufs["hftok"][bb] = t_hl
                    if hsl == 3:
                        t_hready = t_hl
            t_dlast = None
            for dm in range(DCH):
                pi = 4 + dm % 2
                t_d = None
                for ecl in range(4):
                    w0 = [pstok[pi], tdld[ecl], t_hready] if ecl == 0 else []
                    p.op("tensor", lambda e, pi=pi, ecl=ecl, dm=dm: e.matmul(
                        PS[pi][:, :], tdb[:, ecl * 2048 + dm * 128: ecl * 2048 + (dm + 1) * 128],
                        hgh[:, ecl * 512:(ecl + 1) * 512], start=(ecl == 0), stop=False),
                        waits=w0)
                    t_d = p.op("tensor", lambda e, pi=pi, ecl=ecl, dm=dm: e.matmul(
                        PS[pi][:, :], tdb[:, ecl * 2048 + dm * 128: ecl * 2048 + (dm + 1) * 128],
                        hgl[:, ecl * 512:(ecl + 1) * 512], start=False, stop=(ecl == 3)),
                        inc=MM)
                t_fl = p.op("vector", lambda e, pi=pi, dm=dm, r=r: e.scalar_tensor_tensor(
                    X[:, dm * TOK + r * RND: dm * TOK + r * RND + RND], PS[pi][:, :],
                    ssc[:, 2:3], X[:, dm * TOK + r * RND: dm * TOK + r * RND + RND],
                    ALU.mult, ALU.add), waits=[t_d], inc=VE)
                pstok[pi] = t_fl
                t_dlast = t_d
            tdtok[0] = t_dlast
            htok[0] = t_dlast
        return t_ugfinal

    def ffn_phase(tag):
        ps = ExitStack()
        T = lambda n, s, d: ps.enter_context(nc.sbuf_tensor(n + tag, s, d))
        do_quant = (tag == "ffn1")
        if do_quant:
            qbufs = (T("wld", [128, 2048], F32), T("s1b", [128, 2048], BF16),
                     T("s2b", [128, 2048], BF16), T("tb", [128, 2048], BF16))
            qug = [None] * 16
            qtd = [None] * 16
            qug[0] = quant_tiles(tug_group_tiles(0), qbufs, QW[0])
            qtd[0] = quant_tiles(td_cs_tiles(0), qbufs, QW[2])
            qtd[1] = qtd[0]
            qug[1] = quant_tiles(tug_group_tiles(1), qbufs, QW[1])

            def quant_cb(g):
                if g + 2 <= 15:
                    qug[g + 2] = quant_tiles(tug_group_tiles(g + 2), qbufs, QW[(g + 2) % 2])
                    tiles = td_cs_tiles(g + 2)
                    if tiles:
                        tt = quant_tiles(tiles, qbufs, QW[2 + ((g + 2) // 2) % 2])
                        qtd[g + 2] = tt
                        if g + 3 <= 15:
                            qtd[g + 3] = tt
        else:
            qug = qtd = quant_cb = None
        xnh = T("xnh", [128, DCH * RND], BF16)
        xnl = T("xnl", [128, DCH * RND], BF16)
        ugb0 = T("ugb0", [128, DCH * 512], BF16)
        ugb1 = T("ugb1", [128, DCH * 512], BF16)
        tdb = T("tdb", [128, 4 * 2048], BF16)
        hgh = T("hgh", [128, 4 * 512], BF16)
        hgl = T("hgl", [128, 4 * 512], BF16)
        silub = T("silub", [128, 1024], F32)
        hfb = T("hfb", [128, 1024], F32)
        xt1 = T("xt1", [128, 512], F32)
        sq = xt1
        xt2 = T("xt2", [128, 512], F32)
        mrow = T("mrow", [1, 512], F32)
        vrow = T("vrow", [1, 512], F32)
        srow = T("srow", [1, 512], F32)
        rrow = T("rrow", [1, 512], F32)
        trow = T("trow", [1, 512], F32)
        wrow = T("wrow", [1, 512], F32)
        lnbufs = dict(sq=sq, mrow=mrow, vrow=vrow, srow=srow, rrow=rrow, trow=trow,
                      wrow=wrow, xt1=xt1, xt2=xt2)
        fbufs = dict(ugb=[ugb0, ugb1], tdb=tdb, silub=silub, hfb=hfb, hgh=hgh, hgl=hgl,
                     ugtok=[None, None], tdtok=[None], htok=[None],
                     silutok=[None, None], hftok=[None, None])
        t_prev = ()
        for r in range(2):
            t_xn = ln_emit(xnh, xnl, r, lambda dc: dc * RND, lnbufs, t_extra=t_prev)
            t_ug = ffn_emit(r, xnh, xnl, t_xn, fbufs,
                            quant_cb=quant_cb if (do_quant and r == 0) else None,
                            qug=qug, qtd=qtd)
            t_prev = (t_ug,)
        ps.close()
        p.barrier()

    ffn_phase("ffn1")
    if "ffn1" in d_dbg:
        p.dma("gpsimd", d_dbg["ffn1"][:, :], X[:, :], SG)
        p.dma("gpsimd", d_dbg_tug[:, :], d_tug[0:2048, :], SG)
        p.dma("gpsimd", d_dbg_td[:, :], d_td[:, 0:1024], SG)

    ps_qkv = ExitStack()
    if _STOP_AFTER != "ffn1":
        Tq = lambda n, s, d: ps_qkv.enter_context(nc.sbuf_tensor(n, s, d))
        xnh2 = Tq("xnh2", [128, DCH * TOK], BF16)
        xnl2 = Tq("xnl2", [128, DCH * TOK], BF16)
        wqf = Tq("wqf", [128, 2 * 384], F32)
        wqh = Tq("wqh", [128, 2 * 384], BF16)
        wql = Tq("wql", [128, 2 * 384], BF16)
        qf = Tq("qf", [128, 2 * 1024], F32)
        sqq = Tq("sqq", [128, 1024], F32)
        qhib = Tq("qhib", [128, 1024], BF16)
        qlob = Tq("qlob", [128, 1024], BF16)
        qrow = Tq("qrow", [1, 512], F32)
        sq2 = Tq("sq2", [128, 512], F32)
        xt1b = Tq("xt1b", [128, 512], F32)
        xt2b = Tq("xt2b", [128, 512], F32)
        m2 = Tq("m2", [1, 512], F32)
        v2 = Tq("v2", [1, 512], F32)
        s2r = Tq("s2r", [1, 512], F32)
        r2 = Tq("r2", [1, 512], F32)
        t2r = Tq("t2r", [1, 512], F32)
        w2r = Tq("w2r", [1, 512], F32)
        lnbufs2 = dict(sq=sq2, mrow=m2, vrow=v2, srow=s2r, rrow=r2, trow=t2r, wrow=w2r,
                       xt1=xt1b, xt2=xt2b)
        t_xn2a = ln_emit(xnh2, xnl2, 0, lambda dc: dc * TOK, lnbufs2)
        t_xn2b = ln_emit(xnh2, xnl2, 1, lambda dc: dc * TOK + RND, lnbufs2)

        wtok = [None, None]
        sgl = None
        qftok = None
        for og in range(6):
            mmend = [None] * 6
            for dc in range(DCH):
                b = dc % 2
                t_l = p.dma("sync", wqf[:, b * 384:(b + 1) * 384],
                            d_wqkv[dc * 128:(dc + 1) * 128, og * 384:(og + 1) * 384], SY,
                            waits=[wtok[b]])
                t_cp = p.op("vector", lambda e, b=b: e.tensor_copy(wqh[:, b * 384:(b + 1) * 384],
                                                                    wqf[:, b * 384:(b + 1) * 384]),
                            waits=[t_l], inc=VE)
                t_sp = p.op("vector", lambda e, b=b: e.tensor_tensor(
                    wql[:, b * 384:(b + 1) * 384], wqf[:, b * 384:(b + 1) * 384],
                    wqh[:, b * 384:(b + 1) * 384], ALU.subtract), waits=[t_cp], inc=VE)
                for ocl in range(3):
                    for th in range(2):
                        pi = ocl * 2 + th
                        woff = b * 384 + ocl * 128
                        rhsh = xnh2[:, dc * TOK + th * RND: dc * TOK + th * RND + RND]
                        rhsl = xnl2[:, dc * TOK + th * RND: dc * TOK + th * RND + RND]
                        w0 = [pstok[pi], t_sp, t_xn2a, t_xn2b] if dc == 0 else [t_sp]
                        p.op("tensor", lambda e, pi=pi, woff=woff, rhsh=rhsh, dc=dc: e.matmul(
                            PS[pi][:, :], wqh[:, woff:woff + 128], rhsh,
                            start=(dc == 0), stop=False), waits=w0)
                        p.op("tensor", lambda e, pi=pi, woff=woff, rhsl=rhsl: e.matmul(
                            PS[pi][:, :], wqh[:, woff:woff + 128], rhsl,
                            start=False, stop=False))
                        mmend[pi] = p.op("tensor", lambda e, pi=pi, woff=woff, rhsh=rhsh, dc=dc: e.matmul(
                            PS[pi][:, :], wql[:, woff:woff + 128], rhsh,
                            start=False, stop=(dc == DCH - 1)), inc=MM)
                wtok[b] = mmend[5]
            for ocl in range(3):
                oc = og * 3 + ocl
                t_ev = [None, None]
                for th in range(2):
                    pi = ocl * 2 + th
                    sc_ = ISQD if oc < H else 1.0
                    t_ev[th] = p.op("scalar", lambda e, pi=pi, th=th, oc=oc, sc_=sc_: e.activation(
                        qf[:, th * 512: th * 512 + 512],
                        PS[pi][:, :], AF.Identity, bias=bqs[:, oc:oc + 1], scale=sc_),
                        waits=[mmend[pi], t_bq, qftok], inc=ACt)
                    pstok[pi] = t_ev[th]
                if oc < H:
                    t_hi = p.op("vector", lambda e: e.tensor_copy(qhib[:, :], qf[:, 0:1024]),
                                waits=[t_ev[0], t_ev[1], sgl], inc=VE)
                    t_lo = p.op("vector", lambda e: e.tensor_tensor(
                        qlob[:, :], qf[:, 0:1024], qhib[:, :], ALU.subtract), inc=VE)
                    qftok = t_lo
                    t_sqq = p.op("scalar", lambda e: e.square(sqq[:, :], qf[:, 0:1024]),
                                 waits=[t_ev[0], t_ev[1]], inc=ACt)
                    sgl4 = []
                    for t in range(2):
                        t_qn = p.op("tensor", lambda e, t=t: e.matmul(
                            PS[6][0:1, :], ones_cf[:, :], sqq[:, t * 512:(t + 1) * 512],
                            start=True, stop=True), waits=[t_sqq, pstok[6]], inc=MM)
                        t_qr = p.op("scalar", lambda e: e.copy(qrow[:, :], PS[6][0:1, :]),
                                    waits=[t_qn], inc=ACt)
                        pstok[6] = t_qr
                        sgl4.append(p.dma("gpsimd", d_qn[oc:oc + 1, t * 512:(t + 1) * 512], qrow[:, :],
                                          SG, waits=[t_qr]))
                        sgl4.append(p.dma("gpsimd", d_qsp[0, t, oc], qhib[:, t * 512:(t + 1) * 512],
                                          SG, waits=[t_hi]))
                        sgl4.append(p.dma("gpsimd", d_qsp[1, t, oc], qlob[:, t * 512:(t + 1) * 512],
                                          SG, waits=[t_lo]))
                    sgl = sgl4[-1]
                elif oc == H:
                    t_hi = p.op("vector", lambda e: e.tensor_copy(ownKh[:, :], qf[:, 0:1024]),
                                waits=[t_ev[0], t_ev[1]], inc=VE)
                    t_lo = p.op("vector", lambda e: e.tensor_tensor(
                        ownKl[:, :], qf[:, 0:1024], ownKh[:, :], ALU.subtract), inc=VE)
                    qftok = t_lo
                    p.dma("gpsimd", d_kvsh[0], ownKh[:, :], SG, waits=[t_hi])
                    sgl = p.dma("gpsimd", d_kvsh[1], ownKl[:, :], SG, waits=[t_lo])
                else:
                    tvc = None
                    for c in range(8):
                        pi = 6 + c % 2
                        t_tp = p.op("tensor", lambda e, c=c, pi=pi: e.transpose(
                            PS[pi][:, 0:128], qf[:, c * 128:(c + 1) * 128], ident[:, :]),
                            waits=[t_ev[0], t_ev[1], t_id, pstok[pi]], inc=MM)
                        tvc = p.op("scalar", lambda e, c=c, pi=pi: e.copy(
                            sqq[:, c * 128:(c + 1) * 128], PS[pi][:, 0:128]),
                            waits=[t_tp], inc=ACt)
                        pstok[pi] = tvc
                    qftok = tvc
                    t_hi = p.op("vector", lambda e: e.tensor_copy(ownVh[:, :], sqq[:, :]),
                                waits=[tvc], inc=VE)
                    t_lo = p.op("vector", lambda e: e.tensor_tensor(
                        ownVl[:, :], sqq[:, :], ownVh[:, :], ALU.subtract), inc=VE)
                    p.dma("gpsimd", d_kvsh[2], ownVh[:, :], SG, waits=[t_hi])
                    sgl = p.dma("gpsimd", d_kvsh[3], ownVl[:, :], SG, waits=[t_lo])
        if _DBG:
            p.dma("gpsimd", d_dbg_xn2[:, :], xnh2[:, :], SG, waits=[t_xn2b])
            p.dma("gpsimd", d_dbg_wqh[:, :], wqh[:, :], SG, waits=[(VE, VE.n)])
        t_gather = p.op("gpsimd",
                        lambda e: e.collective_compute(
                            "AllGather", ALU.bypass, replica_groups=[list(range(NC))],
                            ins=[d_kvsh[:, :, :]], outs=[d_kvall[:, :, :]]),
                        waits=[(SG, SG.n)], inc=CC)
    ps_qkv.close()
    p.barrier()
    if "qkv" in d_dbg:
        p.dma("gpsimd", d_dbg["qkv"][:, :], X[:, :], SG)

    ps_att = ExitStack()
    if _STOP_AFTER not in ("ffn1", "qkv"):
        Ta = lambda n, s, d: ps_att.enter_context(nc.sbuf_tensor(n, s, d))
        qhi = Ta("qhi", [128, H * 512], BF16)
        qlo = Ta("qlo", [128, H * 512], BF16)
        Ohi = Ta("Ohi", [128, H * 512], BF16)
        Olo = Ta("Olo", [128, H * 512], BF16)
        quad = Ta("quad", [128, 2 * 2048], BF16)
        Pf = Ta("Pf", [128, 2 * 512], F32)
        Phi = Ta("Phi", [128, 2 * 512], BF16)
        Plo = Ta("Plo", [128, 2 * 512], BF16)
        Bl = Ta("Bl", [128, 512], F32)
        Oft = Ta("Oft", [128, 512], F32)
        dm4 = Ta("dm4", [128, 4 * 512], BF16)
        padc = Ta("padc", [128, 192], F32)
        negst = Ta("negst", [1, 2 * 512], BF16)
        qnst = Ta("qnst", [1, 2 * 512], F32)
        qns2 = Ta("qns2", [1, 2 * 512], F32)
        lrow = Ta("lrow", [1, 2 * 512], F32)
        linv = Ta("linv", [1, 2 * 512], F32)
        wfcf = Ta("wfcf", [128, 2048], F32)
        wfh = Ta("wfh", [128, 2 * 2048], BF16)
        wfl = Ta("wfl", [128, 2 * 2048], BF16)
        tfc = Ta("tfc", [128, 512], F32)
        t_dm = p.dma("sync", dm4[:, :], d_dmask[:, :, :].transpose([1, 0, 2]), SY)
        t_pc = p.dma("sync", padc[:, :], d_padc[:, :], SY)
        t_k2 = None
        t_mx = None
        kcur = 0
        for r8 in range(NC):
            b = r8 % 2
            t_l = p.dma("gpsimd", quad[:, b * 1024:(b + 1) * 1024], d_kvall[4 * r8 + 0], SG,
                        waits=[t_gather, t_k2])
            t_k2 = p.op("vector", lambda e, b=b: e.tensor_tensor(
                Phi[:, :], quad[:, b * 1024:(b + 1) * 1024], quad[:, b * 1024:(b + 1) * 1024],
                ALU.mult), waits=[t_l], inc=VE)
            for hf in range(2):
                t_kn = p.op("tensor", lambda e, hf=hf: e.matmul(
                    PS[6][0:1, :], ones_cb[:, :], Phi[:, hf * 512:(hf + 1) * 512],
                    start=True, stop=True), waits=[t_k2, pstok[6]], inc=MM)
                t_rm = p.op("vector", lambda e: e.reduce_max(cells[:, 4:5], PS[6][0:1, :], AX.X),
                            waits=[t_kn] + ([t_mx] if t_mx else []), inc=VE)
                pstok[6] = t_rm
                t_mx = p.op("vector", lambda e, kcur=kcur: e.tensor_tensor(
                    cells[:, 1 - kcur:2 - kcur], cells[:, kcur:kcur + 1], cells[:, 4:5], ALU.max),
                    waits=[t_rm], inc=VE)
                kcur = 1 - kcur
        t_sqk = p.op("scalar", lambda e, kcur=kcur: e.activation(cells[:, 2:3], cells[:, kcur:kcur + 1],
                                                                 AF.Sqrt), waits=[t_mx], inc=ACt)
        t_nk = p.op("vector", lambda e: e.tensor_single_scalar(cells[:, 3:4], cells[:, 2:3],
                                                               -1.012, ALU.mult),
                    waits=[t_sqk], inc=VE)
        if _DBG:
            p.dma("gpsimd", d_dbg_kc[:, :], cells[:, :], SG, waits=[t_nk])
            p.dma("gpsimd", d_dbg_qn[:, :], d_qn[:, :], SG, waits=[t_nk])
            p.dma("gpsimd", d_dbg_kv[:, :], d_kvall[2], SG, waits=[t_nk])
            p.dma("gpsimd", d_dbg_ownk[:, :], ownKh[:, :], SG, waits=[t_nk])

        t_ODone = [None] * H
        t_pvlast = None
        t_fcw = None
        t_fcprev = None
        ucnt = 0
        quadtok = [t_k2, t_k2]
        ptok = [None, None]
        sptok = [None, None]
        negtok = [None, None]
        negcons = [None, None]
        for t in range(2):
            t_qh = p.dma("sync", qhi[:, :], d_qsp[0, t].transpose([1, 0, 2]), SY,
                         waits=[t_pvlast])
            t_ql = p.dma("sync", qlo[:, :], d_qsp[1, t].transpose([1, 0, 2]), SY)
            nq = RS[t] // 4
            for hp in range(H // 2):
                heads = (2 * hp, 2 * hp + 1)
                t_ld2 = None
                for hh, h in enumerate(heads):
                    t_ld2 = p.dma("gpsimd", qnst[:, hh * 512:(hh + 1) * 512],
                                  d_qn[h:h + 1, t * 512:(t + 1) * 512], SG,
                                  waits=[negtok[hh]])
                for hh, h in enumerate(heads):
                    t_s = p.op("scalar", lambda e, hh=hh: e.activation(
                        qns2[:, hh * 512:(hh + 1) * 512], qnst[:, hh * 512:(hh + 1) * 512],
                        AF.Sqrt), waits=[t_ld2], inc=ACt)
                    negtok[hh] = p.op("vector", lambda e, hh=hh: e.tensor_scalar(
                        negst[:, hh * 512:(hh + 1) * 512], qns2[:, hh * 512:(hh + 1) * 512],
                        cells[:, 3:4], None, ALU.mult), waits=[t_s, t_nk, negcons[hh]], inc=VE)
                if _DBG and t == 0 and hp == 0:
                    p.dma("gpsimd", d_dbg_neg[:, :], negst[:, :], SG, waits=[negtok[0], negtok[1]])
                Ot = [PS[0], PS[1]]
                lt = [PS[2], PS[3]]
                first = [True, True]
                t_lmm = [None, None]
                t_pv = [None, None]

                ud = []
                for qd in range(nq):
                    qb = qd % 2
                    q0 = qb * 2048
                    for hh, h in enumerate(heads):
                        for j in range(4):
                            u = 4 * qd + j
                            ud.append((hh, h,
                                       quad[:, q0 + j * 128: q0 + (j + 1) * 128],
                                       quad[:, q0 + 512 + j * 128: q0 + 512 + (j + 1) * 128],
                                       quad[:, q0 + 1024 + j * 128: q0 + 1024 + (j + 1) * 128],
                                       quad[:, q0 + 1536 + j * 128: q0 + 1536 + (j + 1) * 128],
                                       t * 96 + u, None,
                                       qd if (hh == 0 and j == 0) else None, qd))
                for hh, h in enumerate(heads):
                    for di in range(4):
                        c0 = t * 512 + di * 128
                        ud.append((hh, h,
                                   ownKh[:, c0:c0 + 128], ownKl[:, c0:c0 + 128],
                                   ownVh[:, c0:c0 + 128], ownVl[:, c0:c0 + 128],
                                   t * 96 + RS[t] + di, di, None, None))
                stage_tok = {}
                quad_last_pv = {}
                pend = {}

                def emit_S(u):
                    nonlocal ucnt
                    hh, h, kh_ap, kl_ap, vh_ap, vl_ap, bias_col, mask_di, sqd, qd = ud[u]
                    if sqd is not None:
                        qb = sqd % 2
                        rq = sqd if sqd < 8 else 15 - sqd
                        coff = 0 if sqd < 8 else 512
                        stage_tok[sqd] = p.dma(
                            "sync", quad[:, qb * 2048:(qb + 1) * 2048],
                            d_kvall[4 * rq:4 * rq + 4, :, coff:coff + 512].transpose([1, 0, 2]),
                            SY, waits=[quadtok[qb], t_gather])
                    t_stage = stage_tok.get(qd)
                    sb_ = 4 + ucnt % 2
                    pb = ucnt % 2
                    ucnt += 1
                    w0 = [pstok[sb_], t_qh, t_ql, negtok[hh], t_stage]
                    p.op("tensor", lambda e, sb_=sb_, kh_ap=kh_ap, h=h: e.matmul(
                        PS[sb_][:, :], kh_ap, qhi[:, h * 512:(h + 1) * 512],
                        start=True, stop=False), waits=w0)
                    p.op("tensor", lambda e, sb_=sb_, kh_ap=kh_ap, h=h: e.matmul(
                        PS[sb_][:, :], kh_ap, qlo[:, h * 512:(h + 1) * 512],
                        start=False, stop=False))
                    p.op("tensor", lambda e, sb_=sb_, kl_ap=kl_ap, h=h: e.matmul(
                        PS[sb_][:, :], kl_ap, qhi[:, h * 512:(h + 1) * 512],
                        start=False, stop=False))
                    t_S = p.op("tensor", lambda e, sb_=sb_, hh=hh: e.matmul(
                        PS[sb_][:, :], ones_rb[:, :], negst[:, hh * 512:(hh + 1) * 512],
                        start=False, stop=True), inc=MM)
                    negcons[hh] = t_S
                    if mask_di is not None:
                        t_S = p.op("vector", lambda e, sb_=sb_, mask_di=mask_di: e.tensor_tensor(
                            PS[sb_][:, :], PS[sb_][:, :], dm4[:, mask_di * 512:(mask_di + 1) * 512],
                            ALU.add), waits=[t_S, t_dm], inc=VE)
                    pend[u] = (sb_, pb, t_S)

                def emit_T(u):
                    nonlocal t_pvlast
                    hh, h, kh_ap, kl_ap, vh_ap, vl_ap, bias_col, mask_di, sqd, qd = ud[u]
                    sb_, pb, t_S = pend.pop(u)
                    t_e = p.op("scalar", lambda e, sb_=sb_, pb=pb, bias_col=bias_col: e.activation(
                        Pf[:, pb * 512:(pb + 1) * 512], PS[sb_][:, :], AF.Exp,
                        bias=padc[:, bias_col:bias_col + 1]),
                        waits=[t_S, sptok[pb], t_pc], inc=ACt)
                    pstok[sb_] = t_e
                    p.op("vector", lambda e, pb=pb: e.tensor_copy(
                        Phi[:, pb * 512:(pb + 1) * 512], Pf[:, pb * 512:(pb + 1) * 512]),
                        waits=[t_e, ptok[pb]])
                    t_sp = p.op("vector", lambda e, pb=pb: e.tensor_tensor(
                        Plo[:, pb * 512:(pb + 1) * 512], Pf[:, pb * 512:(pb + 1) * 512],
                        Phi[:, pb * 512:(pb + 1) * 512], ALU.subtract), inc=VE)
                    sptok[pb] = t_sp
                    st0 = first[hh]
                    w1 = [t_sp] + ([pstok[2 + hh], pstok[hh]] if st0 else [])
                    p.op("tensor", lambda e, hh=hh, pb=pb, st0=st0: e.matmul(
                        lt[hh][0:1, :], ones_cb[:, :], Phi[:, pb * 512:(pb + 1) * 512],
                        start=st0, stop=False), waits=w1)
                    t_lmm[hh] = p.op("tensor", lambda e, hh=hh, pb=pb: e.matmul(
                        lt[hh][0:1, :], ones_cb[:, :], Plo[:, pb * 512:(pb + 1) * 512],
                        start=False, stop=False), inc=MM)
                    p.op("tensor", lambda e, hh=hh, pb=pb, vh_ap=vh_ap, st0=st0: e.matmul(
                        Ot[hh][:, :], vh_ap, Phi[:, pb * 512:(pb + 1) * 512],
                        start=st0, stop=False))
                    p.op("tensor", lambda e, hh=hh, pb=pb, vh_ap=vh_ap: e.matmul(
                        Ot[hh][:, :], vh_ap, Plo[:, pb * 512:(pb + 1) * 512],
                        start=False, stop=False))
                    t_pv[hh] = p.op("tensor", lambda e, hh=hh, pb=pb, vl_ap=vl_ap: e.matmul(
                        Ot[hh][:, :], vl_ap, Phi[:, pb * 512:(pb + 1) * 512],
                        start=False, stop=False), inc=MM)
                    ptok[pb] = t_pv[hh]
                    first[hh] = False
                    t_pvlast = t_pv[hh]
                    if qd is not None and u % 8 == 7:
                        quadtok[qd % 2] = t_pv[hh]

                emit_S(0)
                for u in range(len(ud)):
                    if u + 1 < len(ud):
                        emit_S(u + 1)
                    emit_T(u)
                t_om = None
                for hh, h in enumerate(heads):
                    t_lr = p.op("scalar", lambda e, hh=hh: e.copy(
                        lrow[:, hh * 512:(hh + 1) * 512], lt[hh][0:1, :]),
                        waits=[t_lmm[hh], t_pv[hh]], inc=ACt)
                    if _DBG and t == 0 and hp == 0 and hh == 1:
                        p.dma("gpsimd", d_dbg_lrow[:, :], lrow[:, :], SG, waits=[t_lr])
                    pstok[2 + hh] = t_lr
                    t_li = p.op("vector", lambda e, hh=hh: e.reciprocal(
                        linv[:, hh * 512:(hh + 1) * 512], lrow[:, hh * 512:(hh + 1) * 512]),
                        waits=[t_lr], inc=VE)
                    pbk = 6 + hh
                    t_B = p.op("tensor", lambda e, hh=hh, pbk=pbk: e.matmul(
                        PS[pbk][:, :], ones_rf[:, :], linv[:, hh * 512:(hh + 1) * 512],
                        start=True, stop=True), waits=[t_li, pstok[pbk]], inc=MM)
                    t_Bc = p.op("scalar", lambda e, pbk=pbk: e.copy(
                        Bl[:, :], PS[pbk][:, :]), waits=[t_B, t_om], inc=ACt)
                    pstok[pbk] = t_Bc
                    t_om = p.op("vector", lambda e, hh=hh: e.tensor_tensor(
                        Oft[:, :], Ot[hh][:, :], Bl[:, :], ALU.mult), waits=[t_Bc], inc=VE)
                    pstok[hh] = t_om
                    p.op("vector", lambda e, h=h: e.tensor_copy(
                        Ohi[:, h * 512:(h + 1) * 512], Oft[:, :]),
                        waits=[t_fcprev])
                    t_ODone[h] = p.op("vector", lambda e, h=h: e.tensor_tensor(
                        Olo[:, h * 512:(h + 1) * 512], Oft[:, :],
                        Ohi[:, h * 512:(h + 1) * 512], ALU.subtract), inc=VE)
            if _DBG:
                p.dma("gpsimd", d_dbg_ohi[t], Ohi[:, :], SG, waits=[t_ODone[H - 1]])
                p.dma("gpsimd", d_dbg_olo[t], Olo[:, :], SG, waits=[t_ODone[H - 1]])
            t_fcmm_prev = None
            t_tfc_free = None
            for hg in range(8):
                heads = (2 * hg, 2 * hg + 1)
                t_ws = [None, None]
                for hh, h in enumerate(heads):
                    t_wl = p.dma("sync", wfcf[:, :], d_wfc[h * 128:(h + 1) * 128, :], SY,
                                 waits=[t_fcw, t_fcmm_prev])
                    p.op("vector", lambda e, hh=hh: e.tensor_copy(
                        wfh[:, hh * 2048:(hh + 1) * 2048], wfcf[:, :]),
                        waits=[t_wl, t_fcmm_prev])
                    t_ws[hh] = p.op("vector", lambda e, hh=hh: e.tensor_tensor(
                        wfl[:, hh * 2048:(hh + 1) * 2048], wfcf[:, :],
                        wfh[:, hh * 2048:(hh + 1) * 2048], ALU.subtract), inc=VE)
                    t_fcw = t_ws[hh]
                for ec in range(DCH):
                    pi = ec % 2
                    t_mmf = None
                    for hh, h in enumerate(heads):
                        w0 = [pstok[pi], t_ws[hh], t_ODone[h]] if hh == 0 else [t_ws[hh], t_ODone[h]]
                        p.op("tensor", lambda e, pi=pi, hh=hh, h=h, ec=ec: e.matmul(
                            PS[pi][:, :], wfh[:, hh * 2048 + ec * 128: hh * 2048 + (ec + 1) * 128],
                            Ohi[:, h * 512:(h + 1) * 512], start=(hh == 0), stop=False), waits=w0)
                        p.op("tensor", lambda e, pi=pi, hh=hh, h=h, ec=ec: e.matmul(
                            PS[pi][:, :], wfh[:, hh * 2048 + ec * 128: hh * 2048 + (ec + 1) * 128],
                            Olo[:, h * 512:(h + 1) * 512], start=False, stop=False))
                        t_mmf = p.op("tensor", lambda e, pi=pi, hh=hh, h=h, ec=ec: e.matmul(
                            PS[pi][:, :], wfl[:, hh * 2048 + ec * 128: hh * 2048 + (ec + 1) * 128],
                            Ohi[:, h * 512:(h + 1) * 512], start=False, stop=(hh == 1)), inc=MM)
                    xs = X[:, ec * TOK + t * RND: ec * TOK + t * RND + RND]
                    if hg == 0:
                        t_fc = p.op("scalar", lambda e, pi=pi, ec=ec: e.activation(
                            tfc[:, :], PS[pi][:, :], AF.Identity,
                            bias=bfc[:, ec:ec + 1]), waits=[t_mmf, t_bfc, t_tfc_free], inc=ACt)
                        t_fl = p.op("vector", lambda e, pi=pi, xs=xs: e.tensor_tensor(
                            xs, xs, tfc[:, :], ALU.add),
                            waits=[t_fc], inc=VE)
                        pstok[pi] = t_fc
                        t_tfc_free = t_fl
                    else:
                        t_fl = p.op("vector", lambda e, pi=pi, xs=xs: e.tensor_tensor(
                            xs, xs, PS[pi][:, :], ALU.add), waits=[t_mmf], inc=VE)
                        pstok[pi] = t_fl
                    t_fcmm_prev = t_mmf
            t_fcprev = t_fcmm_prev
    ps_att.close()
    p.barrier()
    if "attn" in d_dbg:
        p.dma("gpsimd", d_dbg["attn"][:, :], X[:, :], SG)

    if _STOP_AFTER is None:
        ffn_phase("ffn2")
        if "ffn2" in d_dbg:
            p.dma("gpsimd", d_dbg["ffn2"][:, :], X[:, :], SG)

    with nc.sbuf_tensor("orow", [128, 2 * D], F32) as orow:
        otok = [None, None]
        for r in range(2):
            for tq in range(4):
                b = tq % 2
                t_cp = None
                for dc in range(DCH):
                    pi = 6 + dc % 2
                    t_tp = p.op("tensor", lambda e, r=r, tq=tq, dc=dc, pi=pi: e.transpose(
                        PS[pi][:, 0:128],
                        X[:, dc * TOK + r * RND + tq * 128: dc * TOK + r * RND + (tq + 1) * 128],
                        ident[:, :]), waits=[pstok[pi], otok[b]], inc=MM)
                    t_cp = p.op("scalar", lambda e, b=b, dc=dc, pi=pi: e.copy(
                        orow[:, b * D + dc * 128: b * D + (dc + 1) * 128], PS[pi][:, 0:128]),
                        waits=[t_tp], inc=ACt)
                    pstok[pi] = t_cp
                otok[b] = p.dma("gpsimd", d_out[r * 512 + tq * 128: r * 512 + (tq + 1) * 128, :],
                                orow[:, b * D:(b + 1) * D], SG, waits=[t_cp])
    p.wait("gpsimd", [(SG, SG.n), (SY, SY.n), (SC, SC.n)])

    with es, nc.Block() as block:
        p.build(block)
    return nc, p.nops


def _host_inputs(L0_attn_x, L0_attn_w_qkv, L0_attn_b_qkv, L0_attn_w_fc, L0_attn_b_fc,
                 L0_ffn_W_up, L0_ffn_W_gate, L0_ffn_W_down):
    x = np.ascontiguousarray(np.asarray(L0_attn_x, dtype=np.float32)[0])
    wqkv = np.ascontiguousarray(np.asarray(L0_attn_w_qkv, dtype=np.float32))
    bqkv = np.asarray(L0_attn_b_qkv, dtype=np.float32)
    wfc = np.ascontiguousarray(np.asarray(L0_attn_w_fc, dtype=np.float32))
    bfc_ = np.asarray(L0_attn_b_fc, dtype=np.float32)
    wu = np.ascontiguousarray(np.asarray(L0_ffn_W_up, dtype=np.float32))
    wg = np.ascontiguousarray(np.asarray(L0_ffn_W_gate, dtype=np.float32))
    wd = np.ascontiguousarray(np.asarray(L0_ffn_W_down, dtype=np.float32))

    bqkv2d = np.ascontiguousarray(bqkv.reshape(OCH, 128).T)
    bfc2d = np.ascontiguousarray(bfc_.reshape(DCH, 128).T)
    ident = np.eye(128, dtype=np.float32)
    identb = np.eye(128).astype(BF)
    mrange = np.arange(128)[:, None]
    jrange = np.arange(RND)[None, :]
    dmask = np.stack([np.where(128 * oi + mrange > jrange, np.float32(NEG), np.float32(0.0))
                      for oi in range(4)]).astype(BF)

    in_maps = []
    for i in range(NC):
        rows = np.concatenate([x[512 * i:512 * (i + 1)], x[512 * (15 - i):512 * (16 - i)]], axis=0)
        padc = np.zeros((128, 192), dtype=np.float32)
        for t, blk in ((0, i), (1, 15 - i)):
            nreal = 4 * blk
            for u in range(RS[t]):
                if u >= nreal:
                    padc[:, t * 96 + u] = NEG
        im = {
            "x": np.ascontiguousarray(rows),
            "wqkv": wqkv, "bqkv2d": bqkv2d, "wfc": wfc, "bfc2d": bfc2d,
            "wu": wu, "wg": wg, "wd": wd,
            "wu_sh": np.ascontiguousarray(wu[1024 * i:1024 * (i + 1)]),
            "wg_sh": np.ascontiguousarray(wg[1024 * i:1024 * (i + 1)]),
            "wd_sh": np.ascontiguousarray(wd[256 * i:256 * (i + 1)]),
            "ident": ident, "identb": identb, "dmask": dmask,
            "padcol": padc,
        }
        in_maps.append(im)
    return in_maps


def kernel(**inputs):
    global _last_results
    in_maps = _host_inputs(**inputs)
    nc, nops = build_nc()
    res = run_bass_kernel_spmd(nc, in_maps, core_ids=list(range(NC)))
    _last_results = res
    out = np.empty((SEQ, D), dtype=np.float32)
    for i in range(NC):
        xo = np.asarray(res.results[i]["xout"])
        out[512 * i:512 * (i + 1)] = xo[:512]
        out[512 * (15 - i):512 * (16 - i)] = xo[512:]
    return out[None, :, :]
